# version 43
# baseline (speedup 1.0000x reference)
import numpy as np
import concourse.bass as bass
import concourse.mybir as mybir
import concourse.bass_utils as _bu
from concourse.bass_utils import run_bass_kernel_spmd


F32 = mybir.dt.float32
BF16 = mybir.dt.bfloat16
AF = mybir.ActivationFunctionType
ALU = mybir.AluOpType
AX = mybir.AxisListType

T_TOK = 2048
D = 2048
NT = 16
NH = 16
HIDDEN = 8192
NHT = 64
KC_K = 8
KC_C = 256
SEG = 512
NSEG = 4
NHC = 8
EPS = 1e-5

COMPUTE_SEMS = ("T", "V", "S", "G")
DMA_SEMS = ("QC", "QX0", "QX1", "QX2", "QX3", "QO0", "QO1", "QTAP",
            "GC", "GM0", "GM1", "GM2", "GM3", "GM4", "GM5", "GM6", "GM7",
            "GS0", "GS1", "GS2", "GS3", "GS4", "QR0", "QR1",
            "QA0", "QA1", "QA2", "QL0", "QL1", "QL2", "QL3",
            "GW0", "GW1", "GW2", "GW3", "GW4", "GW5", "GW6", "GW7",
            "GX0", "GX1", "GX2", "GX3", "QO2", "QO3",
            "QOG0", "QOG1", "QOG2", "QOG3")
ENG_SEMS = COMPUTE_SEMS + DMA_SEMS


class Res:
    __slots__ = ("writer", "readers", "name")

    def __init__(self, name=""):
        self.writer = None
        self.readers = []
        self.name = name


class MK:
    def __init__(self):
        self.ops = {e: [] for e in "TVSQG"}
        self.cnt = {s: 0 for s in ENG_SEMS}
        self.know = {e: {s: 0 for s in ENG_SEMS} for e in "TVSQG"}
        self.pend_reads = {e: [] for e in "TVSQG"}
        self.pend_writes = {e: [] for e in "TVSQG"}

    def emit(self, eng, fn, reads=(), writes=(), inc=True, dma_sem=None):
        waits = {}
        mysem = dma_sem if dma_sem else eng

        def need(dep):
            if dep is None:
                return
            s2, c2 = dep
            if eng == "T" and s2 == "T":
                return
            if self.know[eng].get(s2, 0) >= c2:
                return
            waits[s2] = max(waits.get(s2, 0), c2)

        for r in reads:
            need(r.writer)
        for w in writes:
            need(w.writer)
            for rd in w.readers:
                need(rd)
        for s2, c2 in waits.items():
            self.know[eng][s2] = c2

        if inc:
            self.cnt[mysem] += 1
            myc = self.cnt[mysem]
            for r in list(reads) + self.pend_reads[eng]:
                r.readers.append((mysem, myc))
            for w in list(writes) + self.pend_writes[eng]:
                w.writer = (mysem, myc)
                w.readers = []
            self.pend_reads[eng] = []
            self.pend_writes[eng] = []
        else:
            self.pend_reads[eng].extend(reads)
            self.pend_writes[eng].extend(writes)
        self.ops[eng].append((waits, fn, inc, dma_sem))
        return (mysem, self.cnt[mysem])

    def wait_now(self, eng, sems):
        waits = {s: self.cnt[s] for s in sems
                 if self.cnt[s] > self.know[eng].get(s, 0)}
        for s2, c2 in waits.items():
            self.know[eng][s2] = c2
        if waits:
            self.ops[eng].append((waits, None, False, False))

    def replay(self, block, sems):
        def make_body(ek):
            oplist = self.ops[ek]

            def body(eng):
                for waits, fn, inc, dma_sem in oplist:
                    for s2, c2 in sorted(waits.items()):
                        v = c2 * 16 if s2 in DMA_SEMS else c2
                        eng.wait_ge(sems[s2], v)
                    if fn is None:
                        continue
                    instr = fn(eng)
                    if inc:
                        s = dma_sem if dma_sem else ek
                        instr.then_inc(sems[s], 16 if dma_sem else 1)
            return body

        block.tensor(make_body("T"))
        block.vector(make_body("V"))
        block.scalar(make_body("S"))
        block.sync(make_body("Q"))
        block.gpsimd(make_body("G"))


def build_program(taps=()):
    nc = bass.Bass("TRN2", target_bir_lowering=False)
    taps = set(taps)

    d_x = nc.dram_tensor("x", [T_TOK, D], F32, kind="ExternalInput").ap()
    d_out = nc.dram_tensor("out", [T_TOK, D], F32, kind="ExternalOutput").ap()
    d_inw = nc.dram_tensor("in_w_p", [NT, 128, NT, 128], F32, kind="ExternalInput").ap()
    d_outw = nc.dram_tensor("out_w_p", [NT, 128, NT, 128], F32, kind="ExternalInput").ap()
    d_w1 = nc.dram_tensor("w1_p", [NHT, 128, NT, 128], F32, kind="ExternalInput").ap()
    d_w2 = nc.dram_tensor("w2", [HIDDEN, D], F32, kind="ExternalInput").ap()
    d_wa = nc.dram_tensor("wa_p", [8, 128, NT, 128], F32, kind="ExternalInput").ap()
    d_wb = nc.dram_tensor("wb_p", [NT, 128, 8, 128], F32, kind="ExternalInput").ap()
    d_mats = nc.dram_tensor("mats_p", [128, NH, 5, 128], F32, kind="ExternalInput").ap()
    d_rw = nc.dram_tensor("router_p", [128, NT, 8], F32, kind="ExternalInput").ap()
    d_ln1w = nc.dram_tensor("ln1w_p", [128, NT], F32, kind="ExternalInput").ap()
    d_ln1b = nc.dram_tensor("ln1b_p", [128, NT], F32, kind="ExternalInput").ap()
    d_ln2w = nc.dram_tensor("ln2w_p", [128, NT], F32, kind="ExternalInput").ap()
    d_ln2b = nc.dram_tensor("ln2b_p", [128, NT], F32, kind="ExternalInput").ap()
    d_inb = nc.dram_tensor("inb_p", [128, NT], F32, kind="ExternalInput").ap()
    d_b1 = nc.dram_tensor("b1_p", [128, NHT], F32, kind="ExternalInput").ap()
    d_outb = nc.dram_tensor("outb_r", [1, D], F32, kind="ExternalInput").ap()
    d_outbp = nc.dram_tensor("outb_p", [128, NT], F32, kind="ExternalInput").ap()
    d_b2 = nc.dram_tensor("b2_r", [1, D], F32, kind="ExternalInput").ap()

    d_rsp = nc.dram_tensor("r_spill", [128, 8 * D], BF16).ap()

    d_taps = {}
    for name in taps:
        shp = [1, 8 * T_TOK] if name == "combine" else [128, NT * D]
        d_taps[name] = nc.dram_tensor("tap_" + name, shp, BF16,
                                      kind="ExternalOutput").ap()

    from contextlib import ExitStack
    es = ExitStack()
    sbt = lambda n, sh, dt: es.enter_context(nc.sbuf_tensor(n, sh, dt))
    pst = lambda n, sh, dt: es.enter_context(nc.psum_tensor(n, sh, dt))

    bufA = sbt("bufA", [128, NT * D], BF16)
    bufB = sbt("bufB", [128, NT * D], BF16)
    pool1 = sbt("pool1", [128, 12288], BF16)
    pool2 = sbt("pool2", [128, 16384], BF16)
    ident_b = sbt("ident_b", [128, 128], BF16)
    ident_f = sbt("ident_f", [128, 128], F32)
    ones_b = sbt("ones_b", [1, SEG], BF16)
    ones_f = sbt("ones_f", [1, 128], F32)
    onescol_b = sbt("onescol_b", [128, 1], BF16)
    rwb = sbt("rwb", [128, NT * 8], BF16)
    cb8 = sbt("cb8", [8, T_TOK], BF16)
    sel8 = sbt("sel8", [8, 1024], BF16)
    ctm = sbt("ctm", [128, 128], F32)
    ctmb = sbt("ctmb", [128, 128], BF16)
    ln1w_c = sbt("ln1w_c", [128, NT], F32)
    ln1b_cb = sbt("ln1b_cb", [128, NT], BF16)
    ln2w_c = sbt("ln2w_c", [128, NT], F32)
    ln2b_c = sbt("ln2b_c", [128, NT], F32)
    inb_c = sbt("inb_c", [128, NT], F32)
    b1_c = sbt("b1_c", [128, NHT], F32)
    outb_rb = sbt("outb_rb", [1, D], BF16)
    outb_c = sbt("outb_c", [128, NT], F32)
    b2_rb = sbt("b2_rb", [1, D], BF16)
    ubias_c = sbt("ubias_c", [128, NT], F32)
    cstf = sbt("cstf", [128, 4], F32)
    stc = [sbt(f"stc{i}", [128, 8], F32) for i in range(6)]
    rtc = [sbt(f"rtc{i}", [128, 8], F32) for i in range(6)]
    ones_row128 = sbt("ones_row128", [128, 128], BF16)
    mk8 = [sbt(f"mk8_{i}", [128, 8], F32) for i in range(12)]

    pt = [pst("pt_a", [128, 2048], F32), pst("pt_b", [128, 2048], F32)]
    RB = [[Res(f"ps{t}b{b}") for b in range(4)] for t in range(2)]

    mk = MK()
    A_k2 = [[Res(f"A{k}h{h}") for h in range(2)] for k in range(NT)]

    def Ak(k, half=None):
        if half is None:
            return A_k2[k]
        return [A_k2[k][half]]

    A_k = [None] * NT
    B_k = [Res(f"B{k}") for k in range(NT)]
    P1S = [Res(f"p1_{i}") for i in range(96)]
    P2S = [Res(f"p2_{i}") for i in range(16)]
    _res = {}

    def R(name):
        if name not in _res:
            _res[name] = Res(name)
        return _res[name]

    def p1r(c0, c1):
        return [P1S[i] for i in range(c0 // 128, (c1 + 127) // 128)]

    def p2r(c0, c1):
        return [P2S[i] for i in range(c0 // 1024, (c1 + 1023) // 1024)]

    def dma(q, out, in_, sem, reads=(), writes=()):
        return mk.emit(q, lambda e, o=out, i=in_: e.dma_start(out=o, in_=i),
                       reads=reads, writes=writes, dma_sem=sem)

    def batch_fixup(res_list, sem):
        for r in res_list:
            if r.writer is not None and r.writer[0] == sem:
                r.writer = (sem, mk.cnt[sem])

    def pbanks(ti, c0, c1):
        return [RB[ti][b] for b in range(c0 // 512, (c1 + 511) // 512)]

    cres = {}
    def CR(n):
        if n not in cres:
            cres[n] = Res("c_" + n)
        return cres[n]

    for nm, dst, src in (("ln1w", ln1w_c, d_ln1w), ("ln2w", ln2w_c, d_ln2w),
                         ("ln2b", ln2b_c, d_ln2b), ("inb", inb_c, d_inb),
                         ("b1", b1_c, d_b1), ("outbc", outb_c, d_outbp)):
        dma("Q", dst[:, :], src[:, :], "QC", writes=[CR(nm)])
    dma("G", ln1b_cb[:, :], d_ln1b[:, :], "GC", writes=[CR("ln1b")])
    dma("G", rwb[:, :], d_rw.rearrange("p k e -> p (k e)"), "GC", writes=[CR("rw")])
    dma("G", outb_rb[:, :], d_outb[:, :], "GC", writes=[CR("outb")])
    dma("G", b2_rb[:, :], d_b2[:, :], "GC", writes=[CR("b2")])
    batch_fixup([CR(n) for n in ("ln1w", "ln2w", "ln2b", "inb", "b1",
                                 "outbc")], "QC")
    batch_fixup([CR(n) for n in ("ln1b", "rw", "outb", "b2")], "GC")
    mk.emit("G", lambda e: e.memset(ones_b[:, :], 1.0), writes=[CR("ones")])
    mk.emit("G", lambda e: e.memset(sel8[:, :], 1.0), writes=[CR("sel8")])
    mk.emit("G", lambda e: e.affine_select(
        out=sel8[:, :], in_=sel8[:, :], compare_op=ALU.is_ge,
        fill=0.0, base=0, pattern=[[1, 1024]], channel_multiplier=-128),
        writes=[CR("sel8")])
    mk.emit("G", lambda e: e.affine_select(
        out=sel8[:, :], in_=sel8[:, :], compare_op=ALU.is_ge,
        fill=0.0, base=127, pattern=[[-1, 1024]], channel_multiplier=128),
        writes=[CR("sel8")])
    mk.emit("G", lambda e: e.memset(onescol_b[:, :], 1.0), writes=[CR("onescol")])
    mk.emit("V", lambda e: e.memset(ones_f[:, :], 1.0), writes=[CR("onesf")])
    mk.emit("V", lambda e: e.memset(ones_row128[:, :], 1.0), writes=[CR("ones128")])
    mk.emit("V", lambda e: e.memset(cstf[:, 0:1], EPS), writes=[CR("cstf")])
    mk.emit("V", lambda e: e.memset(cstf[:, 1:2], -0.5), writes=[CR("cstf")])
    mk.emit("V", lambda e: e.memset(cstf[:, 2:3], -1.0), writes=[CR("cstf")])
    nc.const_aps.register(F32, EPS, cstf[:, 0:1])
    nc.const_aps.register(F32, -0.5, cstf[:, 1:2])
    nc.const_aps.register(F32, -1.0, cstf[:, 2:3])
    mk.emit("V", lambda e: e.memset(cstf[:, 3:4], -1.0 / D), writes=[CR("cstf")])
    nc.const_aps.register(F32, -1.0 / D, cstf[:, 3:4])
    for nm, idt in (("identb", ident_b), ("identf", ident_f)):
        mk.emit("G", lambda e, t=idt: e.memset(t[:, :], 0.0), writes=[CR(nm)])
        mk.emit("G", lambda e, t=idt: e.affine_select(
            out=t[:, :], in_=t[:, :], compare_op=ALU.not_equal, fill=1.0,
            base=0, pattern=[[-1, 128]], channel_multiplier=1),
            writes=[CR(nm)])

    NW0 = 6
    st_res = [R(f"st{i}") for i in range(NW0)]
    rCTM = R("ctm")
    rtr = [R(f"rtr{i}") for i in range(NW0)]
    rCB = R("cbfm")
    NINV = -1.0 / D

    def xtile_ap(i):
        return pool2[:, (i % 4) * 4096:(i % 4) * 4096 + 4096].bitcast(F32)

    def xtile_res(i):
        return p2r((i % 4) * 4096, (i % 4) * 4096 + 4096)

    def xntile_ap(i):
        return bufB[:, i * D:(i + 1) * D]

    def xntile_res(i):
        return [B_k[i]]

    _fill_hook = [lambda: None]

    def p0_stage(t, stage):
        xt, xnt, c = xtile_ap(t), xntile_ap(t), stc[t % NW0]
        xr, xnr, st = xtile_res(t), xntile_res(t), st_res[t % NW0]
        if stage == 0:
            dma("Q", xt[:, :], d_x[t * 128:(t + 1) * 128, :], f"QX{t % 4}",
                reads=xr, writes=xr)
        elif stage == 1:
            mk.emit("V", lambda e, xt=xt, c=c: e.reduce_sum(
                c[:, 0:1], xt[:, :], axis=AX.X), reads=xr + [st], writes=[st])
            mk.emit("S", lambda e, xt=xt, xnt=xnt, c=c: e.activation(
                xnt[:, :], xt[:, :], AF.Square, accum_out=c[:, 1:2]),
                reads=xr + [st] + xnr, writes=xnr + [st])
        elif stage == 2:
            mk.emit("V", lambda e, c=c: e.tensor_scalar(
                c[:, 2:3], c[:, 0:1], 1.0 / D, None, ALU.mult),
                reads=[st], writes=[st])
            mk.emit("V", lambda e, c=c: e.scalar_tensor_tensor(
                c[:, 2:3], c[:, 0:1], c[:, 2:3], c[:, 1:2], ALU.mult,
                ALU.subtract), reads=[st], writes=[st])
            mk.emit("V", lambda e, c=c: e.tensor_scalar(
                c[:, 5:6], c[:, 2:3], -1.0, 1e-24, ALU.mult, ALU.max),
                reads=[st], writes=[st])
        elif stage == 3:
            mk.emit("S", lambda e, c=c: e.activation(
                c[:, 3:4], c[:, 2:3], AF.Ln, bias=EPS, scale=NINV),
                reads=[st, CR("cstf")], writes=[st])
            mk.emit("S", lambda e, c=c: e.activation(
                c[:, 5:6], c[:, 5:6], AF.Ln), reads=[st], writes=[st])
            mk.emit("S", lambda e, c=c: e.activation(
                c[:, 4:5], c[:, 3:4], AF.Exp, scale=-0.5),
                reads=[st, CR("cstf")], writes=[st])
        elif stage == 4:
            mk.emit("V", lambda e, c=c: e.tensor_sub(
                c[:, 6:7], c[:, 5:6], c[:, 3:4]), reads=[st], writes=[st])
            mk.emit("V", lambda e, c=c: e.scalar_tensor_tensor(
                c[:, 7:8], c[:, 0:1], NINV, c[:, 4:5], ALU.mult, ALU.mult),
                reads=[st], writes=[st])
        elif stage == 5:
            mk.emit("S", lambda e, c=c: e.activation(
                c[:, 6:7], c[:, 6:7], AF.Exp, scale=-0.5),
                reads=[st, CR("cstf")], writes=[st])
            mk.emit("S", lambda e, xt=xt, xnt=xnt, c=c: e.activation(
                xnt[:, :], xt[:, :], AF.Identity, bias=c[:, 7:8],
                scale=c[:, 4:5]), reads=[st] + xr + xnr, writes=xnr)
        elif stage == 6:
            for g in range(4):
                if t > 0:
                    tm1 = t - 1
                    rps = pbanks(1, 1024 + (tm1 % 2) * 512,
                                 1536 + (tm1 % 2) * 512)
                    rpsum = pt[1][:, 1024 + (tm1 % 2) * 512:
                                  1024 + (tm1 % 2) * 512 + 8]
                    for kk in range(4):
                        k = g * 4 + kk
                        mk.emit("T", lambda e, k=k, tm1=tm1, rpsum=rpsum:
                                e.matmul(
                            rpsum,
                            bufA[:, k * D + tm1 * 128:k * D + tm1 * 128 + 128],
                            rwb[:, k * 8:(k + 1) * 8], start=(k == 0),
                            stop=(k == NT - 1)),
                            reads=Ak(k, tm1 // 8) + [CR("rw")], writes=rps,
                            inc=(k == NT - 1))
                if t <= 8:
                    slot = (t * 4 + g) % 8
                    ptr = pt[0][:, slot * 256:256 + slot * 256].bitcast(BF16)
                    bres = RB[0][slot // 2]
                else:
                    slot = (t * 4 + g) % 4
                    ptr = pt[0][:, 1024 + slot * 256:1280 + slot * 256].bitcast(BF16)
                    bres = RB[0][2 + slot // 2]
                for kk in range(4):
                    k = g * 4 + kk
                    mk.emit("T", lambda e, ptr=ptr, kk=kk, xnt=xnt, k=k:
                            e.transpose(ptr[:, kk * 128:(kk + 1) * 128],
                                        xnt[:, k * 128:(k + 1) * 128],
                                        ident_b[:, :]),
                            reads=xnr + [CR("identb")], writes=[bres],
                            inc=(kk == 3))
                outap = bass.AP(bufA, g * 4 * D + t * 128,
                                [[NT * D, 128], [D, 4], [1, 128]])
                mk.emit("S", lambda e, o=outap, ptr=ptr: e.activation(
                    o, ptr[:, :], AF.Copy), reads=[bres],
                    writes=[A_k2[g * 4 + i][t // 8] for i in range(4)])
                _fill_hook[0]()
        elif stage == 7:
            if t == NT - 1:
                rps = pbanks(1, 1024 + (t % 2) * 512, 1536 + (t % 2) * 512)
                rpsum = pt[1][:, 1024 + (t % 2) * 512:1024 + (t % 2) * 512 + 8]
                for k in range(NT):
                    mk.emit("T", lambda e, k=k, t=t, rpsum=rpsum: e.matmul(
                        rpsum, bufA[:, k * D + t * 128:k * D + t * 128 + 128],
                        rwb[:, k * 8:(k + 1) * 8], start=(k == 0),
                        stop=(k == NT - 1)),
                        reads=Ak(k, t // 8) + [CR("rw")], writes=rps,
                        inc=(k == NT - 1))
            _fill_hook[0]()
        elif stage == 8:
            rps = pbanks(1, 1024 + (t % 2) * 512, 1536 + (t % 2) * 512)
            rpsum = pt[1][:, 1024 + (t % 2) * 512:1024 + (t % 2) * 512 + 8]
            lg, msk1 = mk8[(t % NW0) * 2], mk8[(t % NW0) * 2 + 1]
            rt, rc = rtr[t % NW0], rtc[t % NW0]
            mk.emit("V", lambda e, c=c, lg=lg, rpsum=rpsum: e.tensor_scalar_mul(
                lg[:, :], rpsum, c[:, 6:7]),
                reads=rps + [st, rt], writes=[rt] + rps)
            mk.emit("V", lambda e, lg=lg, rc=rc: e.reduce_max(
                rc[:, 0:1], lg[:, :], axis=AX.X, negate=True),
                reads=[rt], writes=[rt])
            mk.emit("V", lambda e, lg=lg, rc=rc, msk1=msk1: e.tensor_scalar(
                msk1[:, :], lg[:, :], rc[:, 0:1], 0.0, ALU.add, ALU.is_ge),
                reads=[rt], writes=[rt])
            mk.emit("V", lambda e, lg=lg, msk1=msk1: e.scalar_tensor_tensor(
                lg[:, :], msk1[:, :], -2e30, lg[:, :], ALU.mult, ALU.add),
                reads=[rt], writes=[rt])
            mk.emit("V", lambda e, lg=lg, rc=rc: e.reduce_max(
                rc[:, 1:2], lg[:, :], axis=AX.X, negate=True),
                reads=[rt], writes=[rt])
            mk.emit("V", lambda e, lg=lg, rc=rc: e.tensor_scalar(
                lg[:, :], lg[:, :], rc[:, 1:2], 0.0, ALU.add, ALU.is_ge),
                reads=[rt], writes=[rt])
            mk.emit("S", lambda e, rc=rc: e.activation(
                rc[:, 2:3], rc[:, 1:2], AF.Exp, bias=rc[:, 0:1], scale=-1.0),
                reads=[rt, CR("cstf")], writes=[rt])
            mk.emit("V", lambda e, rc=rc: e.tensor_scalar(
                rc[:, 3:4], rc[:, 2:3], 1.0, None, ALU.add),
                reads=[rt], writes=[rt])
            mk.emit("V", lambda e, rc=rc: e.reciprocal(
                rc[:, 4:5], rc[:, 3:4]), reads=[rt], writes=[rt])
            mk.emit("V", lambda e, lg=lg, msk1=msk1: e.tensor_sub(
                msk1[:, :], msk1[:, :], lg[:, :]), reads=[rt], writes=[rt])
            mk.emit("V", lambda e, t=t, lg=lg, msk1=msk1, rc=rc:
                    e.scalar_tensor_tensor(
                ctm[:, t * 8:(t + 1) * 8], msk1[:, :], rc[:, 4:5], lg[:, :],
                ALU.mult, ALU.add), reads=[rt], writes=[rCTM, rt])

    steps = {}
    for t in range(NT):
        for stage in range(9):
            steps.setdefault((3 * t) // 2 + stage + (1 if stage == 8 else 0),
                             []).append((t, stage))
    p0_order = sorted(steps)
    p0_head = [s for s in p0_order if s <= 18]
    p0_tail = [s for s in p0_order if s > 18]

    if "xn" in d_taps:
        pass

    rUB = R("ubias")
    ln1w_full = pool1[:, 6144:8192]
    lwf_r = p1r(6144, 8192)

    def p1_pre_gen():
        for k in range(NT):
            mk.emit("V", lambda e, k=k: e.tensor_scalar(
                ln1w_full[:, k * 128:(k + 1) * 128], ones_row128[:, :],
                ln1w_c[:, k:k + 1], None, ALU.mult),
                reads=[CR("ln1w"), CR("ones128")] + lwf_r, writes=lwf_r,
                inc=(k == NT - 1))

        def pload(m):
            c0 = 8192 + (m % 2) * 2048
            sl, sr = pool1[:, c0:c0 + 2048], p1r(c0, c0 + 2048)
            dma("G", sl[:, :], d_inw[m].rearrange("p k j -> p (k j)"),
                f"GS{3 + m % 2}", reads=sr, writes=sr)

        pload(0)
        pload(1)
        yield
        for m in range(NT):
            c0 = 8192 + (m % 2) * 2048
            sl, sr = pool1[:, c0:c0 + 2048], p1r(c0, c0 + 2048)
            bb = pbanks(1, 512, 1024)
            for k in range(NT):
                mk.emit("T", lambda e, k=k, sl=sl: e.matmul(
                    pt[1][:, 512:513], sl[:, k * 128:(k + 1) * 128],
                    ln1b_cb[:, k:k + 1], start=(k == 0), stop=(k == NT - 1)),
                    reads=sr + [CR("ln1b")], writes=bb, inc=(k == NT - 1))
            mk.emit("V", lambda e, m=m: e.tensor_tensor(
                ubias_c[:, m:m + 1], pt[1][:, 512:513],
                inb_c[:, m:m + 1], ALU.add), reads=bb + [CR("inb")],
                writes=[rUB] + bb)
            if m + 2 < NT:
                pload(m + 2)
            yield

    def p1_load(m):
        c0 = (m % 3) * 2048
        sl, sr = pool1[:, c0:c0 + 2048], p1r(c0, c0 + 2048)
        dma("G", sl[:, :], d_inw[m].rearrange("p k j -> p (k j)"),
            f"GS{m % 3}", reads=sr, writes=sr)

    def p1_prep(m):
        c0 = (m % 3) * 2048
        sl, sr = pool1[:, c0:c0 + 2048], p1r(c0, c0 + 2048)
        mk.emit("V", lambda e, sl=sl: e.tensor_tensor(
            sl[:, :], sl[:, :], ln1w_full[:, :], ALU.mult),
            reads=sr + lwf_r, writes=sr)

    def p1_mms(m, half, ks):
        c0 = (m % 3) * 2048
        sl, sr = pool1[:, c0:c0 + 2048], p1r(c0, c0 + 2048)
        ub = pbanks(m % 2, 0, 1024)
        for k in ks:
            for gg in range(2):
                g = half * 2 + gg
                last = (k == NT - 1 and gg == 1)
                mk.emit("T", lambda e, m=m, k=k, g=g, gg=gg, sl=sl: e.matmul(
                    pt[m % 2][:, gg * 512:(gg + 1) * 512],
                    sl[:, k * 128:(k + 1) * 128],
                    bufA[:, k * D + g * SEG:k * D + (g + 1) * SEG],
                    start=(k == 0), stop=(k == NT - 1)),
                    reads=sr + Ak(k, half), writes=ub, inc=last)

    def p1_evac(m, half):
        ub = pbanks(m % 2, 0, 1024)
        outap = bass.AP(bufB, m * D + half * 128,
                        [[NT * D, 128], [1, 128], [KC_C, KC_K]])
        mk.emit("S", lambda e, m=m, o=outap: e.activation(
            o, pt[m % 2][:, 0:1024], AF.Identity, bias=ubias_c[:, m:m + 1]),
            reads=ub + [rUB], writes=[B_k[m]] + ub)

    def p1_sweep(half):
        p1_load(0)
        p1_load(1)
        p1_load(2)
        p1_prep(0)
        for m in range(NT):
            p1_mms(m, half, range(8))
            yield
            if m + 1 < NT:
                p1_prep(m + 1)
            p1_mms(m, half, range(8, NT))
            yield
            if m + 3 < NT:
                p1_load(m + 3)
            p1_evac(m, half)
            yield

    _g_pre = [None]

    def pump_pre():
        if _g_pre[0] is not None:
            if next(_g_pre[0], "END") == "END":
                _g_pre[0] = None

    for sidx in p0_head:
        for t, stage in steps[sidx]:
            p0_stage(t, stage)
        if sidx == 1:
            _g_pre[0] = p1_pre_gen()
        pump_pre()
    while _g_pre[0] is not None:
        pump_pre()

    gA = p1_sweep(0)
    ti = 0
    for _ in gA:
        for _ in range(2):
            if ti < len(p0_tail):
                for t, stage in steps[p0_tail[ti]]:
                    p0_stage(t, stage)
                ti += 1
    while ti < len(p0_tail):
        for t, stage in steps[p0_tail[ti]]:
            p0_stage(t, stage)
        ti += 1
    if "xn" in d_taps:
        dma("Q", d_taps["xn"][:, :], bufA[:, :], "QTAP",
            reads=[r for p in A_k2 for r in p])
    for _ in p1_sweep(1):
        pass

    if "u" in d_taps:
        dma("Q", d_taps["u"][:, :], bufB[:, :], "QTAP", reads=list(B_k))

    mats_ap = lambda h: pool1[:, (h % 8) * 640:(h % 8) * 640 + 640]
    mats_res = lambda h: p1r((h % 8) * 640, (h % 8) * 640 + 640)
    xxt_ap = lambda h: pool1[:, 5120 + (h % 3) * 512:5120 + (h % 3) * 512 + 512]
    xxt_res = lambda h: p1r(5120 + (h % 3) * 512, 5120 + (h % 3) * 512 + 512)
    P_ap = lambda h: pool1[:, 6656 + (h % 3) * 512:6656 + (h % 3) * 512 + 512]
    P_res = lambda h: p1r(6656 + (h % 3) * 512, 6656 + (h % 3) * 512 + 512)
    H_ap = lambda h: pool1[:, 8192 + (h % 3) * 256:8192 + (h % 3) * 256 + 256]
    H_res = lambda h: p1r(8192 + (h % 3) * 256, 8192 + (h % 3) * 256 + 256)
    G_ap = lambda h: pool1[:, 8960 + (h % 3) * 512:8960 + (h % 3) * 512 + 512]
    G_res = lambda h: p1r(8960 + (h % 3) * 512, 8960 + (h % 3) * 512 + 512)
    L_ap = lambda h: pool2[:, (h % 3) * 2048:(h % 3) * 2048 + 2048]
    L_res = lambda h: p2r((h % 3) * 2048, (h % 3) * 2048 + 2048)
    powT_ap = lambda h: pool2[:, 6144 + (h % 3) * 1024:6144 + (h % 3) * 1024 + 1024]
    powT_res = lambda h: p2r(6144 + (h % 3) * 1024, 6144 + (h % 3) * 1024 + 1024)
    _SLOT3 = [(0, 0), (0, 1024), (1, 0)]

    def sten(h):
        return pt[_SLOT3[h % 3][0]]

    def sbase(h):
        return _SLOT3[h % 3][1]

    def spb(h, lo, hi):
        return pbanks(_SLOT3[h % 3][0], sbase(h) + lo, sbase(h) + hi)

    def mats_dma(h):
        mat, mres = mats_ap(h), mats_res(h)
        dma("G", mat[:, :], d_mats[:, h], f"GM{h % 8}", reads=mres, writes=mres)

    _prep = {}

    def prep_level(h, lvl):
        p10, base = sten(h), sbase(h)
        mat, xxt, powt = mats_ap(h), xxt_ap(h), powT_ap(h)
        mres, xres, pwres = mats_res(h), xxt_res(h), powT_res(h)
        if lvl == 0:
            _prep[h] = (mat[:, 512:640], mat[:, 0:128], mres)
        curX, curXT, curres = _prep[h]
        pba = spb(h, 768, 896)
        pbb = spb(h, 896, 1024)
        mk.emit("T", lambda e, XT=curXT, X=curX, p10=p10, base=base: e.matmul(
            p10[:, base + 768:base + 896], XT, X, start=True, stop=True),
            reads=curres, writes=pba)
        mk.emit("T", lambda e, XT=curXT, X=curX, p10=p10, base=base: e.matmul(
            p10[:, base + 896:base + 1024], X, XT, start=True, stop=True),
            reads=curres, writes=pbb)
        nX = xxt[:, (lvl % 2) * 256:(lvl % 2) * 256 + 128]
        if lvl >= 2:
            nXT = powt[:, (lvl - 2) * 128:(lvl - 1) * 128]
            nXTr = pwres
        else:
            nXT = xxt[:, (lvl % 2) * 256 + 128:(lvl % 2) * 256 + 256]
            nXTr = xres
        mk.emit("V", lambda e, nX=nX, p10=p10, base=base: e.tensor_copy(
            nX, p10[:, base + 768:base + 896]), reads=pba, writes=xres + pba)
        mk.emit("S", lambda e, nXT=nXT, p10=p10, base=base: e.activation(
            nXT, p10[:, base + 896:base + 1024], AF.Copy), reads=pbb,
            writes=nXTr + pbb)
        _prep[h] = (nX, nXT, xres + (pwres if lvl >= 2 else []))

    def ph1_step(h, tau):
        p10, base = sten(h), sbase(h)
        mat, L = mats_ap(h), L_ap(h)
        mres = mats_res(h)
        vb = spb(h, 0, 256)
        mk.emit("T", lambda e, h=h, tau=tau, mat=mat, p10=p10, base=base:
                e.matmul(p10[:, base:base + 256], mat[:, 128:256],
                         bufB[:, h * D + tau * KC_C:h * D + (tau + 1) * KC_C],
                         start=True, stop=(tau == 0)),
                reads=mres + [B_k[h]], writes=vb, inc=(tau == 0))
        if tau > 0:
            mk.emit("T", lambda e, tau=tau, mat=mat, L=L, p10=p10, base=base:
                    e.matmul(p10[:, base:base + 256], mat[:, 0:128],
                             L[:, (tau - 1) * KC_C:tau * KC_C],
                             start=False, stop=True),
                    reads=mres + L_res(h), writes=vb)
        mk.emit("S", lambda e, tau=tau, L=L, p10=p10, base=base: e.activation(
            L[:, tau * KC_C:(tau + 1) * KC_C], p10[:, base:base + 256],
            AF.Copy), reads=vb, writes=L_res(h) + vb)

    _ph2 = {}

    def ph2_step(h, r):
        p10, base = sten(h), sbase(h)
        L, powt, Pb = L_ap(h), powT_ap(h), P_ap(h)
        if r == 1:
            _ph2[h] = (L[:, (KC_K - 1) * KC_C:KC_K * KC_C], L_res(h))
        src2, srcres = _ph2[h]
        d_ = 1 << (r - 1)
        hb = spb(h, 256, 512)
        mk.emit("T", lambda e, r=r, src2=src2, powt=powt, d_=d_, p10=p10,
                base=base: e.matmul(
            p10[:, base + 256:base + 256 + KC_C - d_],
            powt[:, (r - 1) * 128:r * 128], src2[:, 0:KC_C - d_],
            start=True, stop=True), reads=srcres + powT_res(h), writes=hb)
        dst = Pb[:, (r % 2) * 256:(r % 2) * 256 + 256]
        mk.emit("V", lambda e, dst=dst, src2=src2, d_=d_, p10=p10, base=base:
                e.tensor_tensor(dst[:, d_:KC_C],
                                p10[:, base + 256:base + 256 + KC_C - d_],
                                src2[:, d_:KC_C], ALU.add),
                reads=hb + srcres, writes=P_res(h) + hb)
        mk.emit("S", lambda e, dst=dst, src2=src2, d_=d_: e.activation(
            dst[:, 0:d_], src2[:, 0:d_], AF.Copy), reads=srcres,
            writes=P_res(h))
        _ph2[h] = (dst, P_res(h))

    def ph2_fin(h):
        src2, srcres = _ph2[h]
        Hb, hres = H_ap(h), H_res(h)
        mk.emit("V", lambda e, Hb=Hb: e.memset(Hb[:, 0:1], 0.0),
                reads=hres, writes=hres)
        mk.emit("V", lambda e, Hb=Hb, src2=src2: e.tensor_copy(
            Hb[:, 1:KC_C], src2[:, 0:KC_C - 1]), reads=srcres + hres,
            writes=hres)
        _ph2[h] = (Hb, hres)

    _ph3 = {}

    def ph3_step(h, tau):
        p10, base = sten(h), sbase(h)
        mat, L, G = mats_ap(h), L_ap(h), G_ap(h)
        if tau == 0:
            _ph3[h] = _ph2[h]
        prev, prevres = _ph3[h]
        gb = spb(h, 512, 768)
        mk.emit("T", lambda e, mat=mat, prev=prev, p10=p10, base=base:
                e.matmul(p10[:, base + 512:base + 768], mat[:, 0:128],
                         prev[:, 0:KC_C], start=True, stop=True),
                reads=mats_res(h) + prevres, writes=gb)
        gdst = G[:, (tau % 2) * 256:(tau % 2) * 256 + 256]
        mk.emit("S", lambda e, gdst=gdst, p10=p10, base=base: e.activation(
            gdst[:, :], p10[:, base + 512:base + 768], AF.Copy), reads=gb,
            writes=G_res(h) + gb)
        mk.emit("V", lambda e, tau=tau, L=L, gdst=gdst: e.tensor_tensor(
            L[:, tau * KC_C:(tau + 1) * KC_C],
            L[:, tau * KC_C:(tau + 1) * KC_C], gdst[:, :], ALU.add),
            reads=G_res(h) + L_res(h), writes=L_res(h))
        _ph3[h] = (gdst, G_res(h))

    def ph4_seg(h, g):
        p10, base = sten(h), sbase(h)
        mat, L = mats_ap(h), L_ap(h)
        yb = spb(h, 0, 512)
        mk.emit("T", lambda e, mat=mat, L=L, g=g, p10=p10, base=base:
                e.matmul(p10[:, base:base + 512], mat[:, 256:384],
                         L[:, g * 512:(g + 1) * 512], start=True, stop=False),
                reads=mats_res(h) + L_res(h), writes=yb, inc=False)
        mk.emit("T", lambda e, mat=mat, h=h, g=g, p10=p10, base=base:
                e.matmul(p10[:, base:base + 512], mat[:, 384:512],
                         bufB[:, h * D + g * 512:h * D + (g + 1) * 512],
                         start=False, stop=True),
                reads=mats_res(h) + [B_k[h]], writes=yb)
        mk.emit("V", lambda e, h=h, g=g, p10=p10, base=base: e.tensor_copy(
            bufB[:, h * D + g * 512:h * D + (g + 1) * 512],
            p10[:, base:base + 512]), reads=yb, writes=[B_k[h]] + yb)

    def ssm_wave_gen(w):
        wave = [h for h in range(3 * w, min(3 * w + 3, NH))]
        if w == 0:
            for h in range(8):
                mats_dma(h)
        elif 3 * w + 3 <= NH:
            for h in wave:
                if 8 <= h + 3 < NH:
                    mats_dma(h + 3)
        for lvl in range(10):
            for h in wave:
                prep_level(h, lvl)
            yield
        for tau in range(KC_K):
            for h in wave:
                ph1_step(h, tau)
            yield
        for r in range(1, 9):
            for h in wave:
                ph2_step(h, r)
            yield
        for h in wave:
            ph2_fin(h)
        for tau in range(KC_K):
            for h in wave:
                ph3_step(h, tau)
            yield
        for g in range(NSEG):
            for h in wave:
                ph4_seg(h, g)
            yield

    rCTMB = R("ctmb")
    RSP = [[R(f"rsp{i}_{p}") for p in range(2)] for i in range(8)]

    def wa_load(e_, hf):
        c0 = 9216 + ((2 * e_ + hf) % 3) * 1024
        slh, srh = pool2[:, c0:c0 + 1024], p2r(c0, c0 + 1024)
        dma("G", slh[:, :],
            d_wa[e_, :, hf * 8:(hf + 1) * 8, :].rearrange("p k j -> p (k j)"),
            f"QA{(2 * e_ + hf) % 3}", reads=srh, writes=srh)

    def cbbc_ap(g):
        return pool2[:, 12288 + g * 512:12288 + g * 512 + 512]

    def cbbc_res(g):
        return p2r(12288 + g * 512, 12288 + g * 512 + 512)

    def p3a_gen():
        mk.emit("V", lambda e: e.tensor_copy(ctmb[:, :], ctm[:, :]),
                reads=[rCTM], writes=[rCTMB])
        wa_load(0, 0)
        wa_load(0, 1)
        wa_load(1, 0)
        yield
        rowb = pbanks(1, 1024, 2048)
        for rnd in range(2):
            for tt in range(8):
                t = rnd * 8 + tt
                mk.emit("T", lambda e, t=t, tt=tt: e.matmul(
                    pt[1][0:8, 1024 + tt * 128:1024 + (tt + 1) * 128],
                    ctmb[:, t * 8:(t + 1) * 8], ident_b[:, :],
                    start=True, stop=True),
                    reads=[rCTMB, CR("identb")], writes=rowb,
                    inc=(tt == 7))
            mk.emit("V", lambda e, rnd=rnd: e.tensor_copy(
                cb8[0:8, rnd * 1024:(rnd + 1) * 1024],
                pt[1][0:8, 1024:2048]), reads=rowb, writes=[rCB] + rowb)
            yield
        for e_ in range(8):
            if e_ >= 1:
                wa_load(e_, 1)
                if e_ < 7:
                    wa_load(e_ + 1, 0)
                yield
            for gh in range(2):
                for g in (gh, gh + 2):
                    bb = pbanks(1, 1024 + (g % 2) * 512, 1536 + (g % 2) * 512)
                    mk.emit("T", lambda e, e_=e_, g=g: e.matmul(
                        pt[1][:, 1024 + (g % 2) * 512:1536 + (g % 2) * 512],
                        sel8[0:8, e_ * 128:(e_ + 1) * 128],
                        cb8[0:8, g * SEG:(g + 1) * SEG],
                        start=True, stop=True), reads=[rCB, CR("sel8")],
                        writes=bb)
                    mk.emit("S", lambda e, g=g: e.activation(
                        cbbc_ap(g)[:, :],
                        pt[1][:, 1024 + (g % 2) * 512:1536 + (g % 2) * 512],
                        AF.Copy), reads=bb, writes=cbbc_res(g) + bb)
                yield
            for g in range(NSEG):
                gp = g % 2
                rb = pbanks(1, 1024 + gp * 512, 1536 + gp * 512)
                for k in range(NT):
                    hf = k // 8
                    c0 = 9216 + ((2 * e_ + hf) % 3) * 1024
                    slh = pool2[:, c0:c0 + 1024]
                    srh = p2r(c0, c0 + 1024)
                    mk.emit("T", lambda e, gp=gp, k=k, g=g, slh=slh, hf=hf:
                            e.matmul(
                        pt[1][:, 1024 + gp * 512:1024 + gp * 512 + 512],
                        slh[:, (k - hf * 8) * 128:(k - hf * 8 + 1) * 128],
                        bufA[:, k * D + g * SEG:k * D + (g + 1) * SEG],
                        start=(k == 0), stop=(k == NT - 1)),
                        reads=srh + Ak(k), writes=rb, inc=(k == NT - 1))
                    if k % 2 == 1:
                        yield
                stg = pool2[:, 14336 + gp * 1024:14336 + gp * 1024 + 512]
                stgr = p2r(14336 + gp * 1024, 14336 + gp * 1024 + 512)
                mk.emit("V", lambda e, g=g, gp=gp, stg=stg:
                        e.tensor_tensor(
                    stg[:, :],
                    pt[1][:, 1024 + gp * 512:1024 + gp * 512 + 512],
                    cbbc_ap(g)[:, :], ALU.mult),
                    reads=rb + cbbc_res(g), writes=stgr + rb)
                dma("Q", d_rsp[:, e_ * D + g * SEG:e_ * D + (g + 1) * SEG],
                    stg[:, :], f"QR{gp}",
                    reads=stgr, writes=[RSP[e_][gp]])
                yield

    g3 = p3a_gen()
    g3_done = [False]
    acc3 = [0.0]
    N_MICRO = 3 + 7 + 8 * (2 + NSEG * 9)
    N_SSM = 6 * 38 - 2 * 30

    def pump3(n=1):
        for _ in range(n):
            if not g3_done[0] and next(g3, "END") == "END":
                g3_done[0] = True

    rate3 = N_MICRO / 220.0
    for w in range(6):
        for _ in ssm_wave_gen(w):
            acc3[0] += rate3
            while acc3[0] >= 1.0:
                pump3()
                acc3[0] -= 1.0
    while not g3_done[0]:
        pump3()

    if "y" in d_taps:
        dma("Q", d_taps["y"][:, :], bufB[:, :], "QTAP", reads=list(B_k))

    for c_ in range(4):
        rr = p2r(c_ * 4096, (c_ + 1) * 4096)
        dma("Q", pool2[:, c_ * 4096:(c_ + 1) * 4096],
            d_rsp[:, c_ * 4096:(c_ + 1) * 4096], f"QL{c_}",
            reads=[RSP[2 * c_][0], RSP[2 * c_][1],
                   RSP[2 * c_ + 1][0], RSP[2 * c_ + 1][1]],
            writes=rr)

    for m in range(NT):
        c0 = (m % 2) * 2048
        sl, sr = pool1[:, c0:c0 + 2048], p1r(c0, c0 + 2048)
        dma("G", sl[:, :], d_outw[m].rearrange("p k j -> p (k j)"),
            f"GS{m % 2}", reads=sr, writes=sr)
        zb = pbanks(m % 2, 0, 2048)
        for g in range(NSEG):
            mk.emit("T", lambda e, m=m, g=g: e.matmul(
                pt[m % 2][:, g * SEG:(g + 1) * SEG],
                outb_rb[0:1, m * 128:(m + 1) * 128], ones_b[0:1, 0:SEG],
                start=True, stop=False), reads=[CR("outb"), CR("ones")],
                writes=zb, inc=False)
        for k in range(NT):
            for g in range(NSEG):
                mk.emit("T", lambda e, m=m, k=k, g=g, sl=sl: e.matmul(
                    pt[m % 2][:, g * SEG:(g + 1) * SEG],
                    sl[:, k * 128:(k + 1) * 128],
                    bufB[:, k * D + g * SEG:k * D + (g + 1) * SEG],
                    start=False, stop=(k == NT - 1)),
                    reads=sr + [B_k[k]], writes=zb,
                    inc=(k == NT - 1 and g == NSEG - 1))
        inap = bass.AP(pt[m % 2], 0, [[2048, 128], [1, KC_C], [KC_C, KC_K]])
        mk.emit("V", lambda e, m=m, i=inap: e.tensor_tensor(
            bufA[:, m * D:(m + 1) * D], i, bufA[:, m * D:(m + 1) * D],
            ALU.add), reads=zb + Ak(m), writes=Ak(m) + zb)

    if "xi" in d_taps:
        dma("Q", d_taps["xi"][:, :], bufA[:, :], "QTAP", reads=[r for p in A_k2 for r in p])

    xsq = pool1[:, 0:2048]
    xsq_r = p1r(0, 2048)
    rowA = pool1[:, 2048:6144].bitcast(F32)[0:1, :]
    rowA_r = p1r(2048, 6144)
    rowB = pool1[:, 6144:10240].bitcast(F32)[0:1, :]
    rowB_r = p1r(6144, 10240)
    m_bc = pool1[:, 0:2048]
    rs_bc = pool1[:, 10240:12288]
    rs_r = p1r(10240, 12288)

    sum_b = pbanks(0, 0, 2048)
    sq_b = pbanks(1, 0, 2048)
    xsq2 = [pool1[:, 0:2048], pool1[:, 10240:12288]]
    xsq2_r = [p1r(0, 2048), p1r(10240, 12288)]

    def p3c_sum(k):
        for sgi in range(NSEG):
            mk.emit("T", lambda e, k=k, sgi=sgi: e.matmul(
                pt[0][0:1, sgi * SEG:(sgi + 1) * SEG], onescol_b[:, 0:1],
                bufA[:, k * D + sgi * SEG:k * D + (sgi + 1) * SEG],
                start=(k == 0), stop=(k == NT - 1)),
                reads=Ak(k) + [CR("onescol")], writes=sum_b,
                inc=(sgi == NSEG - 1))

    for k in range(4):
        p3c_sum(k)
    for k in range(NT):
        xq, xqr = xsq2[k % 2], xsq2_r[k % 2]
        mk.emit("S", lambda e, k=k, xq=xq: e.activation(
            xq[:, :], bufA[:, k * D:(k + 1) * D], AF.Square),
            reads=Ak(k) + xqr, writes=xqr)
        for sgi in range(NSEG):
            mk.emit("T", lambda e, k=k, sgi=sgi, xq=xq: e.matmul(
                pt[1][0:1, sgi * SEG:(sgi + 1) * SEG], onescol_b[:, 0:1],
                xq[:, sgi * SEG:(sgi + 1) * SEG],
                start=(k == 0), stop=(k == NT - 1)),
                reads=xqr + [CR("onescol")], writes=sq_b,
                inc=(sgi == NSEG - 1))
        if k + 4 < NT:
            p3c_sum(k + 4)
    mk.emit("V", lambda e: e.tensor_scalar(
        rowA[:, :], pt[0][0:1, :], 1.0 / D, None, ALU.mult),
        reads=sum_b + rowA_r, writes=rowA_r + sum_b)
    mk.emit("V", lambda e: e.tensor_copy(rowB[:, :], pt[1][0:1, :]),
            reads=sq_b + rowB_r, writes=rowB_r + sq_b)
    mk.emit("S", lambda e: e.activation(
        pt[1][0:1, :], rowA[:, :], AF.Square), reads=rowA_r, writes=sq_b)
    mk.emit("V", lambda e: e.scalar_tensor_tensor(
        rowB[:, :], rowB[:, :], 1.0 / D, pt[1][0:1, :], ALU.mult,
        ALU.subtract), reads=rowB_r + sq_b, writes=rowB_r + sq_b)
    mk.emit("S", lambda e: e.activation(
        rowB[:, :], rowB[:, :], AF.Ln, bias=EPS), reads=rowB_r + [CR("cstf")], writes=rowB_r)
    mk.emit("S", lambda e: e.activation(
        rowB[:, :], rowB[:, :], AF.Exp, scale=-0.5), reads=rowB_r + [CR("cstf")],
        writes=rowB_r)
    mk.emit("V", lambda e: e.scalar_tensor_tensor(
        rowA[:, :], rowA[:, :], -1.0, rowB[:, :], ALU.mult, ALU.mult),
        reads=rowA_r + rowB_r, writes=rowA_r)
    for src_row, srcres, dst, dres in ((rowA, rowA_r, m_bc, xsq_r),
                                       (rowB, rowB_r, rs_bc, rs_r)):
        for g in range(NSEG):
            gp = g % 2
            bb = pbanks(0, gp * 512, gp * 512 + 512)
            mk.emit("T", lambda e, src_row=src_row, g=g, gp=gp: e.matmul(
                pt[0][:, gp * 512:gp * 512 + 512], ones_f[0:1, 0:128],
                src_row[0:1, g * SEG:(g + 1) * SEG], start=True, stop=True),
                reads=srcres + [CR("onesf")], writes=bb)
            mk.emit("V", lambda e, dst=dst, g=g, gp=gp: e.tensor_copy(
                dst[:, g * SEG:(g + 1) * SEG],
                pt[0][:, gp * 512:gp * 512 + 512]), reads=bb,
                writes=dres + bb)
    def p3d_xln2(k):
        mk.emit("V", lambda e, k=k: e.tensor_tensor(
            bufB[:, k * D:(k + 1) * D], bufA[:, k * D:(k + 1) * D],
            rs_bc[:, :], ALU.mult), reads=Ak(k) + [B_k[k]] + rs_r,
            writes=[B_k[k]])
        mk.emit("V", lambda e, k=k: e.tensor_tensor(
            bufB[:, k * D:(k + 1) * D], bufB[:, k * D:(k + 1) * D],
            m_bc[:, :], ALU.add), reads=[B_k[k]] + xsq_r, writes=[B_k[k]])
        mk.emit("S", lambda e, k=k: e.activation(
            bufB[:, k * D:(k + 1) * D], bufB[:, k * D:(k + 1) * D],
            AF.Identity, bias=ln2b_c[:, k:k + 1], scale=ln2w_c[:, k:k + 1]),
            reads=[B_k[k], CR("ln2w"), CR("ln2b")], writes=[B_k[k]])

    def wb_slot(m):
        if m % 2 == 0:
            return pool1[:, 2048:3072], p1r(2048, 3072), "GS3"
        return pool1[:, 6144:7168], p1r(6144, 7168), "GS4"

    for m in range(NT):
        p3d_xln2(m)
        sl, sr, sem = wb_slot(m)
        dma("G", sl[:, :], d_wb[m].rearrange("p e j -> p (e j)"),
            sem, reads=sr, writes=sr)
        fb = pbanks(m % 2, 0, 2048)
        for e_ in range(8):
            for g in range(NSEG):
                last = (e_ == 7 and g == NSEG - 1)
                mk.emit("T", lambda e, m=m, e_=e_, g=g, sl=sl: e.matmul(
                    pt[m % 2][:, g * SEG:(g + 1) * SEG],
                    sl[:, e_ * 128:(e_ + 1) * 128],
                    pool2[:, e_ * D + g * SEG:e_ * D + (g + 1) * SEG],
                    start=(e_ == 0), stop=(e_ == 7)),
                    reads=sr + p2r(e_ * D + g * SEG, e_ * D + (g + 1) * SEG),
                    writes=fb, inc=last)
        mk.emit("V", lambda e, m=m: e.tensor_tensor(
            bufA[:, m * D:(m + 1) * D], pt[m % 2][:, :],
            bufA[:, m * D:(m + 1) * D], ALU.add),
            reads=fb + Ak(m), writes=Ak(m) + fb)

    if "xln2" in d_taps:
        dma("Q", d_taps["xln2"][:, :], bufB[:, :], "QTAP", reads=list(B_k))

    if "s3" in d_taps:
        dma("Q", d_taps["s3"][:, :], bufA[:, :], "QTAP", reads=[r for p in A_k2 for r in p])

    def w1_load(hc, mh):
        Hh = hc * 8 + mh
        c0 = (mh % 2) * 2048
        sl, sr = pool1[:, c0:c0 + 2048], p1r(c0, c0 + 2048)
        dma("G", sl[:, :], d_w1[Hh].rearrange("p k j -> p (k j)"),
            f"GS{mh % 2}", reads=sr, writes=sr)

    w1_load(0, 0)
    w1_load(0, 1)
    for hc in range(NHC):
        for mh in range(8):
            c0 = (mh % 2) * 2048
            sl, sr = pool1[:, c0:c0 + 2048], p1r(c0, c0 + 2048)
            hb = pbanks(mh % 2, 0, 2048)
            for k in range(NT):
                for g in range(NSEG):
                    last = (k == NT - 1 and g == NSEG - 1)
                    mk.emit("T", lambda e, mh=mh, k=k, g=g, sl=sl: e.matmul(
                        pt[mh % 2][:, g * SEG:(g + 1) * SEG],
                        sl[:, k * 128:(k + 1) * 128],
                        bufB[:, k * D + g * SEG:k * D + (g + 1) * SEG],
                        start=(k == 0), stop=(k == NT - 1)),
                        reads=sr + [B_k[k]], writes=hb, inc=last)
            h1res = p2r(mh * D, (mh + 1) * D)
            Hh = hc * 8 + mh
            mk.emit("S", lambda e, mh=mh, Hh=Hh: e.activation(
                pool2[:, mh * D:(mh + 1) * D], pt[mh % 2][:, :], AF.Gelu,
                bias=b1_c[:, Hh:Hh + 1]), reads=hb + [CR("b1")] + h1res,
                writes=h1res + hb)
            if mh + 2 < 8:
                w1_load(hc, mh + 2)
        for mhalf in range(2):
            for khl in range(8):
                kh = hc * 8 + khl
                c0 = 4096 + khl * 1024
                sl, sr = pool1[:, c0:c0 + 1024], p1r(c0, c0 + 1024)
                dma("G", sl[:, :],
                    d_w2[kh * 128:(kh + 1) * 128,
                         mhalf * 1024:(mhalf + 1) * 1024],
                    f"GW{khl}", reads=sr, writes=sr)
            if mhalf == 0 and hc + 1 < NHC:
                w1_load(hc + 1, 0)
                w1_load(hc + 1, 1)
            if mhalf == 1 and hc == NHC - 1:
                for t6 in range(8):
                    dma("G", bufB.bitcast(F32)[:, t6 * 2048:(t6 + 1) * 2048],
                        d_x[t6 * 128:(t6 + 1) * 128, :], "GX0",
                        reads=[B_k[2 * t6], B_k[2 * t6 + 1]],
                        writes=[B_k[2 * t6], B_k[2 * t6 + 1]])
                batch_fixup([B_k[i] for i in range(16)], "GX0")
            for ml in range(8):
                m = mhalf * 8 + ml
                mb = pbanks(ml % 2, 0, 2048)
                for khl in range(8):
                    c0 = 4096 + khl * 1024
                    sl, sr = pool1[:, c0:c0 + 1024], p1r(c0, c0 + 1024)
                    stop = (khl == 7 and hc != 0)
                    for g in range(NSEG):
                        last = (g == NSEG - 1)
                        mk.emit("T", lambda e, ml=ml, khl=khl, g=g, sl=sl,
                                stop=stop: e.matmul(
                            pt[ml % 2][:, g * SEG:(g + 1) * SEG],
                            sl[:, ml * 128:(ml + 1) * 128],
                            pool2[:, khl * D + g * SEG:khl * D + (g + 1) * SEG],
                            start=(khl == 0), stop=stop),
                            reads=sr + p2r(khl * D + g * SEG,
                                           khl * D + (g + 1) * SEG),
                            writes=mb, inc=last)
                if hc == 0:
                    for g in range(NSEG):
                        mk.emit("T", lambda e, m=m, g=g, ml=ml: e.matmul(
                            pt[ml % 2][:, g * SEG:(g + 1) * SEG],
                            b2_rb[0:1, m * 128:(m + 1) * 128],
                            ones_b[0:1, 0:SEG], start=False, stop=True),
                            reads=[CR("b2"), CR("ones")], writes=mb,
                            inc=(g == NSEG - 1))
                mk.emit("V", lambda e, m=m, ml=ml: e.tensor_tensor(
                    bufA[:, m * D:(m + 1) * D], pt[ml % 2][:, :],
                    bufA[:, m * D:(m + 1) * D], ALU.add),
                    reads=mb + Ak(m), writes=Ak(m) + mb)

    def x6_ap(t):
        if t < 8:
            return bufB.bitcast(F32)[:, t * 2048:(t + 1) * 2048]
        s = (t - 8) % 3
        return pool1[:, s * 4096:s * 4096 + 4096].bitcast(F32)

    def x6_res(t):
        if t < 8:
            return [B_k[2 * t], B_k[2 * t + 1]]
        s = (t - 8) % 3
        return p1r(s * 4096, s * 4096 + 4096)

    def p6_load(t):
        xt, xr = x6_ap(t), x6_res(t)
        sem = "GX0" if t < 8 else f"GX{1 + (t - 8) % 3}"
        dma("G", xt[:, :], d_x[t * 128:(t + 1) * 128, :], sem,
            reads=xr, writes=xr)

    for t in range(8, 11):
        p6_load(t)
    for t in range(NT):
        xt, xr = x6_ap(t), x6_res(t)
        oc0 = (t % 4) * 4096
        ot = pool2[:, oc0:oc0 + 4096].bitcast(F32)
        orr = p2r(oc0, oc0 + 4096)
        for g in range(4):
            slot = (t * 4 + g) % 4
            ptr = pt[0][:, slot * 256:slot * 256 + 256].bitcast(BF16)
            bres = RB[0][slot // 2]
            for kk in range(4):
                k = g * 4 + kk
                mk.emit("T", lambda e, ptr=ptr, kk=kk, k=k, t=t: e.transpose(
                    ptr[:, kk * 128:(kk + 1) * 128],
                    bufA[:, k * D + t * 128:k * D + t * 128 + 128],
                    ident_b[:, :]), reads=Ak(k, t // 8) + [CR("identb")],
                    writes=[bres], inc=(kk == 3))
            mk.emit("V", lambda e, ot=ot, g=g, ptr=ptr, xt=xt: e.tensor_tensor(
                ot[:, g * SEG:(g + 1) * SEG], ptr[:, :],
                xt[:, g * SEG:(g + 1) * SEG], ALU.add),
                reads=[bres] + xr + orr, writes=orr + [bres])
        dma("Q" if t < 12 else "G", d_out[t * 128:(t + 1) * 128, :],
            ot[:, :], f"QO{t % 4}" if t < 12 else f"QOG{t - 12}",
            reads=orr, writes=orr)
        if 11 <= t + 3 < NT:
            p6_load(t + 3)

    mk.wait_now("Q", ("QO0", "QO1", "QO2", "QO3", "QOG0", "QOG1",
                      "QOG2", "QOG3", "QTAP", "QC"))

    sems = {s: es.enter_context(nc.semaphore("sem" + s)) for s in ENG_SEMS}
    block = es.enter_context(nc.Block())
    mk.replay(block, sems)
    es.close()
    return nc


def host_layouts(inputs):
    g = lambda k: np.asarray(inputs[k], dtype=np.float32)
    A, Bm, Cm, Dm = g("L0_attn_A"), g("L0_attn_Bm"), g("L0_attn_Cm"), g("L0_attn_Dm")
    mats = np.stack([A.transpose(0, 2, 1), Bm.transpose(0, 2, 1),
                     Cm.transpose(0, 2, 1), Dm.transpose(0, 2, 1), A], axis=1)
    mats_p = np.ascontiguousarray(mats.transpose(2, 0, 1, 3))
    return {
        "in_w_p": np.ascontiguousarray(
            g("L0_attn_in_w").reshape(NT, 128, NT, 128).transpose(2, 1, 0, 3)),
        "out_w_p": np.ascontiguousarray(
            g("L0_attn_out_w").reshape(NT, 128, NT, 128).transpose(2, 1, 0, 3)),
        "w1_p": np.ascontiguousarray(
            g("L0_attn_w1").reshape(NT, 128, NHT, 128).transpose(2, 1, 0, 3)),
        "w2": np.ascontiguousarray(g("L0_attn_w2")),
        "wa_p": np.ascontiguousarray(
            g("L0_ffn_Wa").reshape(8, NT, 128, 128).transpose(0, 2, 1, 3)),
        "wb_p": np.ascontiguousarray(
            g("L0_ffn_Wb").reshape(8, 128, NT, 128).transpose(2, 1, 0, 3)),
        "mats_p": mats_p,
        "outb_p": np.ascontiguousarray(
            g("L0_attn_out_b").reshape(NT, 128).T),
        "router_p": np.ascontiguousarray(
            g("L0_ffn_router_w").reshape(NT, 128, 8).transpose(1, 0, 2)),
        "ln1w_p": np.ascontiguousarray(g("L0_attn_ln1_w").reshape(NT, 128).T),
        "ln1b_p": np.ascontiguousarray(g("L0_attn_ln1_b").reshape(NT, 128).T),
        "ln2w_p": np.ascontiguousarray(g("L0_attn_ln2_w").reshape(NT, 128).T),
        "ln2b_p": np.ascontiguousarray(g("L0_attn_ln2_b").reshape(NT, 128).T),
        "inb_p": np.ascontiguousarray(g("L0_attn_in_b").reshape(NT, 128).T),
        "b1_p": np.ascontiguousarray(g("L0_attn_b1").reshape(NHT, 128).T),
        "outb_r": np.ascontiguousarray(g("L0_attn_out_b").reshape(1, D)),
        "b2_r": np.ascontiguousarray(g("L0_attn_b2").reshape(1, D)),
    }


def kernel(**inputs):
    x = np.asarray(inputs["L0_attn_x"], dtype=np.float32)
    w = host_layouts(inputs)
    nc = build_program()
    in_maps = []
    for i in range(8):
        m = {"x": np.ascontiguousarray(x[i])}
        m.update(w)
        in_maps.append(m)
    res = run_bass_kernel_spmd(nc, in_maps, core_ids=list(range(8)))
    return np.stack([res.results[i]["out"] for i in range(8)], axis=0)


if __name__ == "__main__":
    build_program()
    print("build ok")


# version 44
# speedup vs baseline: 1.1883x; 1.1883x over previous
import numpy as np
import concourse.bass as bass
import concourse.mybir as mybir
import concourse.bass_utils as _bu
from concourse.bass_utils import run_bass_kernel_spmd


F32 = mybir.dt.float32
BF16 = mybir.dt.bfloat16
AF = mybir.ActivationFunctionType
ALU = mybir.AluOpType
AX = mybir.AxisListType

T_TOK = 2048
D = 2048
NT = 16
NH = 16
HIDDEN = 8192
NHT = 64
KC_K = 8
KC_C = 256
SEG = 512
NSEG = 4
NHC = 8
EPS = 1e-5

COMPUTE_SEMS = ("T", "V", "S", "G")
DMA_SEMS = ("QC", "QX0", "QX1", "QX2", "QX3", "QO0", "QO1", "QTAP",
            "GC", "GM0", "GM1", "GM2", "GM3", "GM4", "GM5", "GM6", "GM7",
            "GS0", "GS1", "GS2", "GS3", "GS4", "QR0", "QR1",
            "QA0", "QA1", "QA2", "QL0", "QL1", "QL2", "QL3",
            "GW0", "GW1", "GW2", "GW3", "GW4", "GW5", "GW6", "GW7",
            "GX0", "GX1", "GX2", "GX3", "QO2", "QO3",
            "QOG0", "QOG1", "QOG2", "QOG3")
ENG_SEMS = COMPUTE_SEMS + DMA_SEMS


class Res:
    __slots__ = ("writer", "readers", "name")

    def __init__(self, name=""):
        self.writer = None
        self.readers = []
        self.name = name


class MK:
    def __init__(self):
        self.ops = {e: [] for e in "TVSQG"}
        self.cnt = {s: 0 for s in ENG_SEMS}
        self.know = {e: {s: 0 for s in ENG_SEMS} for e in "TVSQG"}
        self.pend_reads = {e: [] for e in "TVSQG"}
        self.pend_writes = {e: [] for e in "TVSQG"}

    def emit(self, eng, fn, reads=(), writes=(), inc=True, dma_sem=None):
        waits = {}
        mysem = dma_sem if dma_sem else eng

        def need(dep):
            if dep is None:
                return
            s2, c2 = dep
            if eng == "T" and s2 == "T":
                return
            if self.know[eng].get(s2, 0) >= c2:
                return
            waits[s2] = max(waits.get(s2, 0), c2)

        for r in reads:
            need(r.writer)
        for w in writes:
            need(w.writer)
            for rd in w.readers:
                need(rd)
        for s2, c2 in waits.items():
            self.know[eng][s2] = c2

        if inc:
            self.cnt[mysem] += 1
            myc = self.cnt[mysem]
            for r in list(reads) + self.pend_reads[eng]:
                r.readers.append((mysem, myc))
            for w in list(writes) + self.pend_writes[eng]:
                w.writer = (mysem, myc)
                w.readers = []
            self.pend_reads[eng] = []
            self.pend_writes[eng] = []
        else:
            self.pend_reads[eng].extend(reads)
            self.pend_writes[eng].extend(writes)
        self.ops[eng].append((waits, fn, inc, dma_sem))
        return (mysem, self.cnt[mysem])

    def wait_now(self, eng, sems):
        waits = {s: self.cnt[s] for s in sems
                 if self.cnt[s] > self.know[eng].get(s, 0)}
        for s2, c2 in waits.items():
            self.know[eng][s2] = c2
        if waits:
            self.ops[eng].append((waits, None, False, False))

    def replay(self, block, sems):
        def make_body(ek):
            oplist = self.ops[ek]

            def body(eng):
                for waits, fn, inc, dma_sem in oplist:
                    for s2, c2 in sorted(waits.items()):
                        v = c2 * 16 if s2 in DMA_SEMS else c2
                        eng.wait_ge(sems[s2], v)
                    if fn is None:
                        continue
                    instr = fn(eng)
                    if inc:
                        s = dma_sem if dma_sem else ek
                        instr.then_inc(sems[s], 16 if dma_sem else 1)
            return body

        block.tensor(make_body("T"))
        block.vector(make_body("V"))
        block.scalar(make_body("S"))
        block.sync(make_body("Q"))
        block.gpsimd(make_body("G"))


def build_program(taps=()):
    nc = bass.Bass("TRN2", target_bir_lowering=False)
    taps = set(taps)

    d_x = nc.dram_tensor("x", [T_TOK, D], F32, kind="ExternalInput").ap()
    d_out = nc.dram_tensor("out", [T_TOK, D], F32, kind="ExternalOutput").ap()
    d_inw = nc.dram_tensor("in_w_p", [NT, 128, NT, 128], F32, kind="ExternalInput").ap()
    d_outw = nc.dram_tensor("out_w_p", [NT, 128, NT, 128], F32, kind="ExternalInput").ap()
    d_w1 = nc.dram_tensor("w1_p", [NHT, 128, NT, 128], F32, kind="ExternalInput").ap()
    d_w2 = nc.dram_tensor("w2", [HIDDEN, D], F32, kind="ExternalInput").ap()
    d_wa = nc.dram_tensor("wa_p", [8, 128, NT, 128], F32, kind="ExternalInput").ap()
    d_wb = nc.dram_tensor("wb_p", [NT, 128, 8, 128], F32, kind="ExternalInput").ap()
    d_mats = nc.dram_tensor("mats_p", [128, NH, 5, 128], F32, kind="ExternalInput").ap()
    d_rw = nc.dram_tensor("router_p", [128, NT, 8], F32, kind="ExternalInput").ap()
    d_ln1w = nc.dram_tensor("ln1w_p", [128, NT], F32, kind="ExternalInput").ap()
    d_ln1b = nc.dram_tensor("ln1b_p", [128, NT], F32, kind="ExternalInput").ap()
    d_ln2w = nc.dram_tensor("ln2w_p", [128, NT], F32, kind="ExternalInput").ap()
    d_ln2b = nc.dram_tensor("ln2b_p", [128, NT], F32, kind="ExternalInput").ap()
    d_inb = nc.dram_tensor("inb_p", [128, NT], F32, kind="ExternalInput").ap()
    d_b1 = nc.dram_tensor("b1_p", [128, NHT], F32, kind="ExternalInput").ap()
    d_outb = nc.dram_tensor("outb_r", [1, D], F32, kind="ExternalInput").ap()
    d_outbp = nc.dram_tensor("outb_p", [128, NT], F32, kind="ExternalInput").ap()
    d_b2 = nc.dram_tensor("b2_r", [1, D], F32, kind="ExternalInput").ap()

    d_rsp = nc.dram_tensor("r_spill", [128, 8 * D], BF16).ap()

    d_taps = {}
    for name in taps:
        shp = [1, 8 * T_TOK] if name == "combine" else [128, NT * D]
        d_taps[name] = nc.dram_tensor("tap_" + name, shp, BF16,
                                      kind="ExternalOutput").ap()

    from contextlib import ExitStack
    es = ExitStack()
    sbt = lambda n, sh, dt: es.enter_context(nc.sbuf_tensor(n, sh, dt))
    pst = lambda n, sh, dt: es.enter_context(nc.psum_tensor(n, sh, dt))

    bufA = sbt("bufA", [128, NT * D], BF16)
    bufB = sbt("bufB", [128, NT * D], BF16)
    pool1 = sbt("pool1", [128, 12288], BF16)
    pool2 = sbt("pool2", [128, 16384], BF16)
    ident_b = sbt("ident_b", [128, 128], BF16)
    ident_f = sbt("ident_f", [128, 128], F32)
    ones_b = sbt("ones_b", [1, SEG], BF16)
    ones_f = sbt("ones_f", [1, 128], F32)
    onescol_b = sbt("onescol_b", [128, 1], BF16)
    rwb = sbt("rwb", [128, NT * 8], BF16)
    cb8 = sbt("cb8", [8, T_TOK], BF16)
    sel8 = sbt("sel8", [8, 1024], BF16)
    ctm = sbt("ctm", [128, 128], F32)
    ctmb = sbt("ctmb", [128, 128], BF16)
    ln1w_c = sbt("ln1w_c", [128, NT], F32)
    ln1b_cb = sbt("ln1b_cb", [128, NT], BF16)
    ln2w_c = sbt("ln2w_c", [128, NT], F32)
    ln2b_c = sbt("ln2b_c", [128, NT], F32)
    inb_c = sbt("inb_c", [128, NT], F32)
    b1_c = sbt("b1_c", [128, NHT], F32)
    outb_rb = sbt("outb_rb", [1, D], BF16)
    outb_c = sbt("outb_c", [128, NT], F32)
    b2_rb = sbt("b2_rb", [1, D], BF16)
    ubias_c = sbt("ubias_c", [128, NT], F32)
    cstf = sbt("cstf", [128, 4], F32)
    stc = [sbt(f"stc{i}", [128, 8], F32) for i in range(6)]
    rtc = [sbt(f"rtc{i}", [128, 8], F32) for i in range(6)]
    ones_row128 = sbt("ones_row128", [128, 128], BF16)
    mk8 = [sbt(f"mk8_{i}", [128, 8], F32) for i in range(12)]

    pt = [pst("pt_a", [128, 2048], F32), pst("pt_b", [128, 2048], F32)]
    RB = [[Res(f"ps{t}b{b}") for b in range(4)] for t in range(2)]

    mk = MK()
    A_k2 = [[Res(f"A{k}h{h}") for h in range(2)] for k in range(NT)]

    def Ak(k, half=None):
        if half is None:
            return A_k2[k]
        return [A_k2[k][half]]

    A_k = [None] * NT
    B_k = [Res(f"B{k}") for k in range(NT)]
    P1S = [Res(f"p1_{i}") for i in range(96)]
    P2S = [Res(f"p2_{i}") for i in range(16)]
    _res = {}

    def R(name):
        if name not in _res:
            _res[name] = Res(name)
        return _res[name]

    def p1r(c0, c1):
        return [P1S[i] for i in range(c0 // 128, (c1 + 127) // 128)]

    def p2r(c0, c1):
        return [P2S[i] for i in range(c0 // 1024, (c1 + 1023) // 1024)]

    def dma(q, out, in_, sem, reads=(), writes=()):
        return mk.emit(q, lambda e, o=out, i=in_: e.dma_start(out=o, in_=i),
                       reads=reads, writes=writes, dma_sem=sem)

    def batch_fixup(res_list, sem):
        for r in res_list:
            if r.writer is not None and r.writer[0] == sem:
                r.writer = (sem, mk.cnt[sem])

    def pbanks(ti, c0, c1):
        return [RB[ti][b] for b in range(c0 // 512, (c1 + 511) // 512)]

    cres = {}
    def CR(n):
        if n not in cres:
            cres[n] = Res("c_" + n)
        return cres[n]

    for nm, dst, src in (("ln1w", ln1w_c, d_ln1w), ("ln2w", ln2w_c, d_ln2w),
                         ("ln2b", ln2b_c, d_ln2b), ("inb", inb_c, d_inb),
                         ("b1", b1_c, d_b1), ("outbc", outb_c, d_outbp)):
        dma("Q", dst[:, :], src[:, :], "QC", writes=[CR(nm)])
    dma("G", ln1b_cb[:, :], d_ln1b[:, :], "GC", writes=[CR("ln1b")])
    dma("G", rwb[:, :], d_rw.rearrange("p k e -> p (k e)"), "GC", writes=[CR("rw")])
    dma("G", outb_rb[:, :], d_outb[:, :], "GC", writes=[CR("outb")])
    dma("G", b2_rb[:, :], d_b2[:, :], "GC", writes=[CR("b2")])
    batch_fixup([CR(n) for n in ("ln1w", "ln2w", "ln2b", "inb", "b1",
                                 "outbc")], "QC")
    batch_fixup([CR(n) for n in ("ln1b", "rw", "outb", "b2")], "GC")
    mk.emit("G", lambda e: e.memset(ones_b[:, :], 1.0), writes=[CR("ones")])
    mk.emit("G", lambda e: e.memset(sel8[:, :], 1.0), writes=[CR("sel8")])
    mk.emit("G", lambda e: e.affine_select(
        out=sel8[:, :], in_=sel8[:, :], compare_op=ALU.is_ge,
        fill=0.0, base=0, pattern=[[1, 1024]], channel_multiplier=-128),
        writes=[CR("sel8")])
    mk.emit("G", lambda e: e.affine_select(
        out=sel8[:, :], in_=sel8[:, :], compare_op=ALU.is_ge,
        fill=0.0, base=127, pattern=[[-1, 1024]], channel_multiplier=128),
        writes=[CR("sel8")])
    mk.emit("G", lambda e: e.memset(onescol_b[:, :], 1.0), writes=[CR("onescol")])
    mk.emit("V", lambda e: e.memset(ones_f[:, :], 1.0), writes=[CR("onesf")])
    mk.emit("V", lambda e: e.memset(ones_row128[:, :], 1.0), writes=[CR("ones128")])
    mk.emit("V", lambda e: e.memset(cstf[:, 0:1], EPS), writes=[CR("cstf")])
    mk.emit("V", lambda e: e.memset(cstf[:, 1:2], -0.5), writes=[CR("cstf")])
    mk.emit("V", lambda e: e.memset(cstf[:, 2:3], -1.0), writes=[CR("cstf")])
    nc.const_aps.register(F32, EPS, cstf[:, 0:1])
    nc.const_aps.register(F32, -0.5, cstf[:, 1:2])
    nc.const_aps.register(F32, -1.0, cstf[:, 2:3])
    mk.emit("V", lambda e: e.memset(cstf[:, 3:4], -1.0 / D), writes=[CR("cstf")])
    nc.const_aps.register(F32, -1.0 / D, cstf[:, 3:4])
    for nm, idt in (("identb", ident_b), ("identf", ident_f)):
        mk.emit("G", lambda e, t=idt: e.memset(t[:, :], 0.0), writes=[CR(nm)])
        mk.emit("G", lambda e, t=idt: e.affine_select(
            out=t[:, :], in_=t[:, :], compare_op=ALU.not_equal, fill=1.0,
            base=0, pattern=[[-1, 128]], channel_multiplier=1),
            writes=[CR(nm)])

    NW0 = 6
    st_res = [R(f"st{i}") for i in range(NW0)]
    rCTM = R("ctm")
    rtr = [R(f"rtr{i}") for i in range(NW0)]
    rCB = R("cbfm")
    NINV = -1.0 / D

    def xtile_ap(i):
        return pool2[:, (i % 4) * 4096:(i % 4) * 4096 + 4096].bitcast(F32)

    def xtile_res(i):
        return p2r((i % 4) * 4096, (i % 4) * 4096 + 4096)

    def xntile_ap(i):
        return bufB[:, i * D:(i + 1) * D]

    def xntile_res(i):
        return [B_k[i]]

    _fill_hook = [lambda: None]

    def p0_stage(t, stage):
        xt, xnt, c = xtile_ap(t), xntile_ap(t), stc[t % NW0]
        xr, xnr, st = xtile_res(t), xntile_res(t), st_res[t % NW0]
        if stage == 0:
            dma("Q", xt[:, :], d_x[t * 128:(t + 1) * 128, :], f"QX{t % 4}",
                reads=xr, writes=xr)
        elif stage == 1:
            mk.emit("V", lambda e, xt=xt, c=c: e.reduce_sum(
                c[:, 0:1], xt[:, :], axis=AX.X), reads=xr + [st], writes=[st])
            mk.emit("S", lambda e, xt=xt, xnt=xnt, c=c: e.activation(
                xnt[:, :], xt[:, :], AF.Square, accum_out=c[:, 1:2]),
                reads=xr + [st] + xnr, writes=xnr + [st])
        elif stage == 2:
            mk.emit("V", lambda e, c=c: e.tensor_scalar(
                c[:, 2:3], c[:, 0:1], 1.0 / D, None, ALU.mult),
                reads=[st], writes=[st])
            mk.emit("V", lambda e, c=c: e.scalar_tensor_tensor(
                c[:, 2:3], c[:, 0:1], c[:, 2:3], c[:, 1:2], ALU.mult,
                ALU.subtract), reads=[st], writes=[st])
            mk.emit("V", lambda e, c=c: e.tensor_scalar(
                c[:, 5:6], c[:, 2:3], -1.0, 1e-24, ALU.mult, ALU.max),
                reads=[st], writes=[st])
        elif stage == 3:
            mk.emit("S", lambda e, c=c: e.activation(
                c[:, 3:4], c[:, 2:3], AF.Ln, bias=EPS, scale=NINV),
                reads=[st, CR("cstf")], writes=[st])
            mk.emit("S", lambda e, c=c: e.activation(
                c[:, 5:6], c[:, 5:6], AF.Ln), reads=[st], writes=[st])
            mk.emit("S", lambda e, c=c: e.activation(
                c[:, 4:5], c[:, 3:4], AF.Exp, scale=-0.5),
                reads=[st, CR("cstf")], writes=[st])
        elif stage == 4:
            mk.emit("V", lambda e, c=c: e.tensor_sub(
                c[:, 6:7], c[:, 5:6], c[:, 3:4]), reads=[st], writes=[st])
            mk.emit("V", lambda e, c=c: e.scalar_tensor_tensor(
                c[:, 7:8], c[:, 0:1], NINV, c[:, 4:5], ALU.mult, ALU.mult),
                reads=[st], writes=[st])
        elif stage == 5:
            mk.emit("S", lambda e, c=c: e.activation(
                c[:, 6:7], c[:, 6:7], AF.Exp, scale=-0.5),
                reads=[st, CR("cstf")], writes=[st])
            mk.emit("S", lambda e, xt=xt, xnt=xnt, c=c: e.activation(
                xnt[:, :], xt[:, :], AF.Identity, bias=c[:, 7:8],
                scale=c[:, 4:5]), reads=[st] + xr + xnr, writes=xnr)
        elif stage == 6:
            for g in range(4):
                if t > 0:
                    tm1 = t - 1
                    rps = pbanks(1, 1024 + (tm1 % 2) * 512,
                                 1536 + (tm1 % 2) * 512)
                    rpsum = pt[1][:, 1024 + (tm1 % 2) * 512:
                                  1024 + (tm1 % 2) * 512 + 8]
                    for kk in range(4):
                        k = g * 4 + kk
                        mk.emit("T", lambda e, k=k, tm1=tm1, rpsum=rpsum:
                                e.matmul(
                            rpsum,
                            bufA[:, k * D + tm1 * 128:k * D + tm1 * 128 + 128],
                            rwb[:, k * 8:(k + 1) * 8], start=(k == 0),
                            stop=(k == NT - 1)),
                            reads=Ak(k, tm1 // 8) + [CR("rw")], writes=rps,
                            inc=(k == NT - 1))
                if t <= 7:
                    slot = (t * 4 + g) % 8
                    ptr = pt[0][:, slot * 256:256 + slot * 256].bitcast(BF16)
                    bres = RB[0][slot // 2]
                else:
                    slot = (t * 4 + g) % 4
                    ptr = pt[0][:, 1024 + slot * 256:1280 + slot * 256].bitcast(BF16)
                    bres = RB[0][2 + slot // 2]
                for kk in range(4):
                    k = g * 4 + kk
                    mk.emit("T", lambda e, ptr=ptr, kk=kk, xnt=xnt, k=k:
                            e.transpose(ptr[:, kk * 128:(kk + 1) * 128],
                                        xnt[:, k * 128:(k + 1) * 128],
                                        ident_b[:, :]),
                            reads=xnr + [CR("identb")], writes=[bres],
                            inc=(kk == 3))
                outap = bass.AP(bufA, g * 4 * D + t * 128,
                                [[NT * D, 128], [D, 4], [1, 128]])
                mk.emit("S", lambda e, o=outap, ptr=ptr: e.activation(
                    o, ptr[:, :], AF.Copy), reads=[bres],
                    writes=[A_k2[g * 4 + i][t // 8] for i in range(4)])
                _fill_hook[0]()
        elif stage == 7:
            if t == NT - 1:
                rps = pbanks(1, 1024 + (t % 2) * 512, 1536 + (t % 2) * 512)
                rpsum = pt[1][:, 1024 + (t % 2) * 512:1024 + (t % 2) * 512 + 8]
                for k in range(NT):
                    mk.emit("T", lambda e, k=k, t=t, rpsum=rpsum: e.matmul(
                        rpsum, bufA[:, k * D + t * 128:k * D + t * 128 + 128],
                        rwb[:, k * 8:(k + 1) * 8], start=(k == 0),
                        stop=(k == NT - 1)),
                        reads=Ak(k, t // 8) + [CR("rw")], writes=rps,
                        inc=(k == NT - 1))
            _fill_hook[0]()
        elif stage == 8:
            rps = pbanks(1, 1024 + (t % 2) * 512, 1536 + (t % 2) * 512)
            rpsum = pt[1][:, 1024 + (t % 2) * 512:1024 + (t % 2) * 512 + 8]
            lg, msk1 = mk8[(t % NW0) * 2], mk8[(t % NW0) * 2 + 1]
            rt, rc = rtr[t % NW0], rtc[t % NW0]
            mk.emit("V", lambda e, c=c, lg=lg, rpsum=rpsum: e.tensor_scalar_mul(
                lg[:, :], rpsum, c[:, 6:7]),
                reads=rps + [st, rt], writes=[rt] + rps)
            mk.emit("V", lambda e, lg=lg, rc=rc: e.reduce_max(
                rc[:, 0:1], lg[:, :], axis=AX.X, negate=True),
                reads=[rt], writes=[rt])
            mk.emit("V", lambda e, lg=lg, rc=rc, msk1=msk1: e.tensor_scalar(
                msk1[:, :], lg[:, :], rc[:, 0:1], 0.0, ALU.add, ALU.is_ge),
                reads=[rt], writes=[rt])
            mk.emit("V", lambda e, lg=lg, msk1=msk1: e.scalar_tensor_tensor(
                lg[:, :], msk1[:, :], -2e30, lg[:, :], ALU.mult, ALU.add),
                reads=[rt], writes=[rt])
            mk.emit("V", lambda e, lg=lg, rc=rc: e.reduce_max(
                rc[:, 1:2], lg[:, :], axis=AX.X, negate=True),
                reads=[rt], writes=[rt])
            mk.emit("V", lambda e, lg=lg, rc=rc: e.tensor_scalar(
                lg[:, :], lg[:, :], rc[:, 1:2], 0.0, ALU.add, ALU.is_ge),
                reads=[rt], writes=[rt])
            mk.emit("S", lambda e, rc=rc: e.activation(
                rc[:, 2:3], rc[:, 1:2], AF.Exp, bias=rc[:, 0:1], scale=-1.0),
                reads=[rt, CR("cstf")], writes=[rt])
            mk.emit("V", lambda e, rc=rc: e.tensor_scalar(
                rc[:, 3:4], rc[:, 2:3], 1.0, None, ALU.add),
                reads=[rt], writes=[rt])
            mk.emit("V", lambda e, rc=rc: e.reciprocal(
                rc[:, 4:5], rc[:, 3:4]), reads=[rt], writes=[rt])
            mk.emit("V", lambda e, lg=lg, msk1=msk1: e.tensor_sub(
                msk1[:, :], msk1[:, :], lg[:, :]), reads=[rt], writes=[rt])
            mk.emit("V", lambda e, t=t, lg=lg, msk1=msk1, rc=rc:
                    e.scalar_tensor_tensor(
                ctm[:, t * 8:(t + 1) * 8], msk1[:, :], rc[:, 4:5], lg[:, :],
                ALU.mult, ALU.add), reads=[rt], writes=[rCTM, rt])

    steps = {}
    for t in range(NT):
        for stage in range(9):
            off = {6: 1, 7: 1, 8: 2}.get(stage, 0)
            steps.setdefault((3 * t) // 2 + stage + off,
                             []).append((t, stage))
    p0_order = sorted(steps)
    p0_head = [s for s in p0_order if s <= 18]
    p0_tail = [s for s in p0_order if s > 18]

    if "xn" in d_taps:
        pass

    rUB = R("ubias")
    ln1w_full = pool1[:, 6144:8192]
    lwf_r = p1r(6144, 8192)

    def p1_pre_gen():
        for k in range(NT):
            mk.emit("V", lambda e, k=k: e.tensor_scalar(
                ln1w_full[:, k * 128:(k + 1) * 128], ones_row128[:, :],
                ln1w_c[:, k:k + 1], None, ALU.mult),
                reads=[CR("ln1w"), CR("ones128")] + lwf_r, writes=lwf_r,
                inc=(k == NT - 1))

        def pload(m):
            c0 = 8192 + (m % 2) * 2048
            sl, sr = pool1[:, c0:c0 + 2048], p1r(c0, c0 + 2048)
            dma("G", sl[:, :], d_inw[m].rearrange("p k j -> p (k j)"),
                f"GS{3 + m % 2}", reads=sr, writes=sr)

        pload(0)
        pload(1)
        yield
        for m in range(NT):
            c0 = 8192 + (m % 2) * 2048
            sl, sr = pool1[:, c0:c0 + 2048], p1r(c0, c0 + 2048)
            bb = pbanks(1, 512, 1024)
            for k in range(NT):
                mk.emit("T", lambda e, k=k, sl=sl: e.matmul(
                    pt[1][:, 512:513], sl[:, k * 128:(k + 1) * 128],
                    ln1b_cb[:, k:k + 1], start=(k == 0), stop=(k == NT - 1)),
                    reads=sr + [CR("ln1b")], writes=bb, inc=(k == NT - 1))
            mk.emit("V", lambda e, m=m: e.tensor_tensor(
                ubias_c[:, m:m + 1], pt[1][:, 512:513],
                inb_c[:, m:m + 1], ALU.add), reads=bb + [CR("inb")],
                writes=[rUB] + bb)
            if m + 2 < NT:
                pload(m + 2)
            yield

    def p1_load(m):
        c0 = (m % 3) * 2048
        sl, sr = pool1[:, c0:c0 + 2048], p1r(c0, c0 + 2048)
        dma("G", sl[:, :], d_inw[m].rearrange("p k j -> p (k j)"),
            f"GS{m % 3}", reads=sr, writes=sr)

    def p1_prep(m):
        c0 = (m % 3) * 2048
        sl, sr = pool1[:, c0:c0 + 2048], p1r(c0, c0 + 2048)
        mk.emit("V", lambda e, sl=sl: e.tensor_tensor(
            sl[:, :], sl[:, :], ln1w_full[:, :], ALU.mult),
            reads=sr + lwf_r, writes=sr)

    def p1_mms(m, half, ks):
        c0 = (m % 3) * 2048
        sl, sr = pool1[:, c0:c0 + 2048], p1r(c0, c0 + 2048)
        ub = pbanks(m % 2, 0, 1024)
        for k in ks:
            for gg in range(2):
                g = half * 2 + gg
                last = (k == NT - 1 and gg == 1)
                mk.emit("T", lambda e, m=m, k=k, g=g, gg=gg, sl=sl: e.matmul(
                    pt[m % 2][:, gg * 512:(gg + 1) * 512],
                    sl[:, k * 128:(k + 1) * 128],
                    bufA[:, k * D + g * SEG:k * D + (g + 1) * SEG],
                    start=(k == 0), stop=(k == NT - 1)),
                    reads=sr + Ak(k, half), writes=ub, inc=last)

    def p1_evac(m, half):
        ub = pbanks(m % 2, 0, 1024)
        outap = bass.AP(bufB, m * D + half * 128,
                        [[NT * D, 128], [1, 128], [KC_C, KC_K]])
        mk.emit("S", lambda e, m=m, o=outap: e.activation(
            o, pt[m % 2][:, 0:1024], AF.Identity, bias=ubias_c[:, m:m + 1]),
            reads=ub + [rUB], writes=[B_k[m]] + ub)

    def p1_sweep(half):
        p1_load(0)
        p1_load(1)
        p1_load(2)
        p1_prep(0)
        for m in range(NT):
            p1_mms(m, half, range(8))
            yield
            if m + 1 < NT:
                p1_prep(m + 1)
            p1_mms(m, half, range(8, NT))
            yield
            if m + 3 < NT:
                p1_load(m + 3)
            p1_evac(m, half)
            yield

    _g_pre = [None]

    def pump_pre():
        if _g_pre[0] is not None:
            if next(_g_pre[0], "END") == "END":
                _g_pre[0] = None

    for sidx in p0_head:
        for t, stage in steps[sidx]:
            p0_stage(t, stage)
        if sidx == 1:
            _g_pre[0] = p1_pre_gen()
        pump_pre()
    while _g_pre[0] is not None:
        pump_pre()

    gA = p1_sweep(0)
    ti = 0
    for _ in gA:
        for _ in range(2):
            if ti < len(p0_tail):
                for t, stage in steps[p0_tail[ti]]:
                    p0_stage(t, stage)
                ti += 1
    while ti < len(p0_tail):
        for t, stage in steps[p0_tail[ti]]:
            p0_stage(t, stage)
        ti += 1
    if "xn" in d_taps:
        dma("Q", d_taps["xn"][:, :], bufA[:, :], "QTAP",
            reads=[r for p in A_k2 for r in p])
    for _ in p1_sweep(1):
        pass

    if "u" in d_taps:
        dma("Q", d_taps["u"][:, :], bufB[:, :], "QTAP", reads=list(B_k))

    mats_ap = lambda h: pool1[:, (h % 8) * 640:(h % 8) * 640 + 640]
    mats_res = lambda h: p1r((h % 8) * 640, (h % 8) * 640 + 640)
    xxt_ap = lambda h: pool1[:, 5120 + (h % 3) * 512:5120 + (h % 3) * 512 + 512]
    xxt_res = lambda h: p1r(5120 + (h % 3) * 512, 5120 + (h % 3) * 512 + 512)
    P_ap = lambda h: pool1[:, 6656 + (h % 3) * 512:6656 + (h % 3) * 512 + 512]
    P_res = lambda h: p1r(6656 + (h % 3) * 512, 6656 + (h % 3) * 512 + 512)
    H_ap = lambda h: pool1[:, 8192 + (h % 3) * 256:8192 + (h % 3) * 256 + 256]
    H_res = lambda h: p1r(8192 + (h % 3) * 256, 8192 + (h % 3) * 256 + 256)
    G_ap = lambda h: pool1[:, 8960 + (h % 3) * 512:8960 + (h % 3) * 512 + 512]
    G_res = lambda h: p1r(8960 + (h % 3) * 512, 8960 + (h % 3) * 512 + 512)
    L_ap = lambda h: pool2[:, (h % 3) * 2048:(h % 3) * 2048 + 2048]
    L_res = lambda h: p2r((h % 3) * 2048, (h % 3) * 2048 + 2048)
    powT_ap = lambda h: pool2[:, 6144 + (h % 3) * 1024:6144 + (h % 3) * 1024 + 1024]
    powT_res = lambda h: p2r(6144 + (h % 3) * 1024, 6144 + (h % 3) * 1024 + 1024)
    _SLOT3 = [(0, 0), (0, 1024), (1, 0)]

    def sten(h):
        return pt[_SLOT3[h % 3][0]]

    def sbase(h):
        return _SLOT3[h % 3][1]

    def spb(h, lo, hi):
        return pbanks(_SLOT3[h % 3][0], sbase(h) + lo, sbase(h) + hi)

    def mats_dma(h):
        mat, mres = mats_ap(h), mats_res(h)
        dma("G", mat[:, :], d_mats[:, h], f"GM{h % 8}", reads=mres, writes=mres)

    _prep = {}

    def prep_level(h, lvl):
        p10, base = sten(h), sbase(h)
        mat, xxt, powt = mats_ap(h), xxt_ap(h), powT_ap(h)
        mres, xres, pwres = mats_res(h), xxt_res(h), powT_res(h)
        if lvl == 0:
            _prep[h] = (mat[:, 512:640], mat[:, 0:128], mres)
        curX, curXT, curres = _prep[h]
        pba = spb(h, 768, 896)
        pbb = spb(h, 896, 1024)
        mk.emit("T", lambda e, XT=curXT, X=curX, p10=p10, base=base: e.matmul(
            p10[:, base + 768:base + 896], XT, X, start=True, stop=True),
            reads=curres, writes=pba)
        mk.emit("T", lambda e, XT=curXT, X=curX, p10=p10, base=base: e.matmul(
            p10[:, base + 896:base + 1024], X, XT, start=True, stop=True),
            reads=curres, writes=pbb)
        nX = xxt[:, (lvl % 2) * 256:(lvl % 2) * 256 + 128]
        if lvl >= 2:
            nXT = powt[:, (lvl - 2) * 128:(lvl - 1) * 128]
            nXTr = pwres
        else:
            nXT = xxt[:, (lvl % 2) * 256 + 128:(lvl % 2) * 256 + 256]
            nXTr = xres
        mk.emit("V", lambda e, nX=nX, p10=p10, base=base: e.tensor_copy(
            nX, p10[:, base + 768:base + 896]), reads=pba, writes=xres + pba)
        mk.emit("S", lambda e, nXT=nXT, p10=p10, base=base: e.activation(
            nXT, p10[:, base + 896:base + 1024], AF.Copy), reads=pbb,
            writes=nXTr + pbb)
        _prep[h] = (nX, nXT, xres + (pwres if lvl >= 2 else []))

    def ph1_step(h, tau):
        p10, base = sten(h), sbase(h)
        mat, L = mats_ap(h), L_ap(h)
        mres = mats_res(h)
        vb = spb(h, 0, 256)
        mk.emit("T", lambda e, h=h, tau=tau, mat=mat, p10=p10, base=base:
                e.matmul(p10[:, base:base + 256], mat[:, 128:256],
                         bufB[:, h * D + tau * KC_C:h * D + (tau + 1) * KC_C],
                         start=True, stop=(tau == 0)),
                reads=mres + [B_k[h]], writes=vb, inc=(tau == 0))
        if tau > 0:
            mk.emit("T", lambda e, tau=tau, mat=mat, L=L, p10=p10, base=base:
                    e.matmul(p10[:, base:base + 256], mat[:, 0:128],
                             L[:, (tau - 1) * KC_C:tau * KC_C],
                             start=False, stop=True),
                    reads=mres + L_res(h), writes=vb)
        mk.emit("S", lambda e, tau=tau, L=L, p10=p10, base=base: e.activation(
            L[:, tau * KC_C:(tau + 1) * KC_C], p10[:, base:base + 256],
            AF.Copy), reads=vb, writes=L_res(h) + vb)

    _ph2 = {}

    def ph2_step(h, r):
        p10, base = sten(h), sbase(h)
        L, powt, Pb = L_ap(h), powT_ap(h), P_ap(h)
        if r == 1:
            _ph2[h] = (L[:, (KC_K - 1) * KC_C:KC_K * KC_C], L_res(h))
        src2, srcres = _ph2[h]
        d_ = 1 << (r - 1)
        hb = spb(h, 256, 512)
        mk.emit("T", lambda e, r=r, src2=src2, powt=powt, d_=d_, p10=p10,
                base=base: e.matmul(
            p10[:, base + 256:base + 256 + KC_C - d_],
            powt[:, (r - 1) * 128:r * 128], src2[:, 0:KC_C - d_],
            start=True, stop=True), reads=srcres + powT_res(h), writes=hb)
        dst = Pb[:, (r % 2) * 256:(r % 2) * 256 + 256]
        mk.emit("V", lambda e, dst=dst, src2=src2, d_=d_, p10=p10, base=base:
                e.tensor_tensor(dst[:, d_:KC_C],
                                p10[:, base + 256:base + 256 + KC_C - d_],
                                src2[:, d_:KC_C], ALU.add),
                reads=hb + srcres, writes=P_res(h) + hb)
        mk.emit("S", lambda e, dst=dst, src2=src2, d_=d_: e.activation(
            dst[:, 0:d_], src2[:, 0:d_], AF.Copy), reads=srcres,
            writes=P_res(h))
        _ph2[h] = (dst, P_res(h))

    def ph2_fin(h):
        src2, srcres = _ph2[h]
        Hb, hres = H_ap(h), H_res(h)
        mk.emit("V", lambda e, Hb=Hb: e.memset(Hb[:, 0:1], 0.0),
                reads=hres, writes=hres)
        mk.emit("V", lambda e, Hb=Hb, src2=src2: e.tensor_copy(
            Hb[:, 1:KC_C], src2[:, 0:KC_C - 1]), reads=srcres + hres,
            writes=hres)
        _ph2[h] = (Hb, hres)

    _ph3 = {}

    def ph3_step(h, tau):
        p10, base = sten(h), sbase(h)
        mat, L, G = mats_ap(h), L_ap(h), G_ap(h)
        if tau == 0:
            _ph3[h] = _ph2[h]
        prev, prevres = _ph3[h]
        gb = spb(h, 512, 768)
        mk.emit("T", lambda e, mat=mat, prev=prev, p10=p10, base=base:
                e.matmul(p10[:, base + 512:base + 768], mat[:, 0:128],
                         prev[:, 0:KC_C], start=True, stop=True),
                reads=mats_res(h) + prevres, writes=gb)
        gdst = G[:, (tau % 2) * 256:(tau % 2) * 256 + 256]
        mk.emit("S", lambda e, gdst=gdst, p10=p10, base=base: e.activation(
            gdst[:, :], p10[:, base + 512:base + 768], AF.Copy), reads=gb,
            writes=G_res(h) + gb)
        mk.emit("V", lambda e, tau=tau, L=L, gdst=gdst: e.tensor_tensor(
            L[:, tau * KC_C:(tau + 1) * KC_C],
            L[:, tau * KC_C:(tau + 1) * KC_C], gdst[:, :], ALU.add),
            reads=G_res(h) + L_res(h), writes=L_res(h))
        _ph3[h] = (gdst, G_res(h))

    def ph4_seg(h, g):
        p10, base = sten(h), sbase(h)
        mat, L = mats_ap(h), L_ap(h)
        yb = spb(h, 0, 512)
        mk.emit("T", lambda e, mat=mat, L=L, g=g, p10=p10, base=base:
                e.matmul(p10[:, base:base + 512], mat[:, 256:384],
                         L[:, g * 512:(g + 1) * 512], start=True, stop=False),
                reads=mats_res(h) + L_res(h), writes=yb, inc=False)
        mk.emit("T", lambda e, mat=mat, h=h, g=g, p10=p10, base=base:
                e.matmul(p10[:, base:base + 512], mat[:, 384:512],
                         bufB[:, h * D + g * 512:h * D + (g + 1) * 512],
                         start=False, stop=True),
                reads=mats_res(h) + [B_k[h]], writes=yb)
        mk.emit("V", lambda e, h=h, g=g, p10=p10, base=base: e.tensor_copy(
            bufB[:, h * D + g * 512:h * D + (g + 1) * 512],
            p10[:, base:base + 512]), reads=yb, writes=[B_k[h]] + yb)

    def ssm_wave_gen(w):
        wave = [h for h in range(3 * w, min(3 * w + 3, NH))]
        if w == 0:
            for h in range(8):
                mats_dma(h)
        elif 3 * w + 3 <= NH:
            for h in wave:
                if 8 <= h + 3 < NH:
                    mats_dma(h + 3)
        for lvl in range(10):
            for h in wave:
                prep_level(h, lvl)
            yield
        for tau in range(KC_K):
            for h in wave:
                ph1_step(h, tau)
            yield
        for r in range(1, 9):
            for h in wave:
                ph2_step(h, r)
            yield
        for h in wave:
            ph2_fin(h)
        for tau in range(KC_K):
            for h in wave:
                ph3_step(h, tau)
            yield
        for g in range(NSEG):
            for h in wave:
                ph4_seg(h, g)
            yield

    rCTMB = R("ctmb")
    RSP = [[R(f"rsp{i}_{p}") for p in range(2)] for i in range(8)]

    def wa_load(e_, hf):
        c0 = 9216 + ((2 * e_ + hf) % 3) * 1024
        slh, srh = pool2[:, c0:c0 + 1024], p2r(c0, c0 + 1024)
        dma("G", slh[:, :],
            d_wa[e_, :, hf * 8:(hf + 1) * 8, :].rearrange("p k j -> p (k j)"),
            f"QA{(2 * e_ + hf) % 3}", reads=srh, writes=srh)

    def cbbc_ap(g):
        return pool2[:, 12288 + g * 512:12288 + g * 512 + 512]

    def cbbc_res(g):
        return p2r(12288 + g * 512, 12288 + g * 512 + 512)

    def p3a_gen():
        mk.emit("V", lambda e: e.tensor_copy(ctmb[:, :], ctm[:, :]),
                reads=[rCTM], writes=[rCTMB])
        wa_load(0, 0)
        wa_load(0, 1)
        wa_load(1, 0)
        yield
        rowb = pbanks(1, 1024, 2048)
        for rnd in range(2):
            for tt in range(8):
                t = rnd * 8 + tt
                mk.emit("T", lambda e, t=t, tt=tt: e.matmul(
                    pt[1][0:8, 1024 + tt * 128:1024 + (tt + 1) * 128],
                    ctmb[:, t * 8:(t + 1) * 8], ident_b[:, :],
                    start=True, stop=True),
                    reads=[rCTMB, CR("identb")], writes=rowb,
                    inc=(tt == 7))
            mk.emit("V", lambda e, rnd=rnd: e.tensor_copy(
                cb8[0:8, rnd * 1024:(rnd + 1) * 1024],
                pt[1][0:8, 1024:2048]), reads=rowb, writes=[rCB] + rowb)
            yield
        for e_ in range(8):
            if e_ >= 1:
                wa_load(e_, 1)
                if e_ < 7:
                    wa_load(e_ + 1, 0)
                yield
            for gh in range(2):
                for g in (gh, gh + 2):
                    bb = pbanks(1, 1024 + (g % 2) * 512, 1536 + (g % 2) * 512)
                    mk.emit("T", lambda e, e_=e_, g=g: e.matmul(
                        pt[1][:, 1024 + (g % 2) * 512:1536 + (g % 2) * 512],
                        sel8[0:8, e_ * 128:(e_ + 1) * 128],
                        cb8[0:8, g * SEG:(g + 1) * SEG],
                        start=True, stop=True), reads=[rCB, CR("sel8")],
                        writes=bb)
                    mk.emit("S", lambda e, g=g: e.activation(
                        cbbc_ap(g)[:, :],
                        pt[1][:, 1024 + (g % 2) * 512:1536 + (g % 2) * 512],
                        AF.Copy), reads=bb, writes=cbbc_res(g) + bb)
                yield
            for g in range(NSEG):
                gp = g % 2
                rb = pbanks(1, 1024 + gp * 512, 1536 + gp * 512)
                for k in range(NT):
                    hf = k // 8
                    c0 = 9216 + ((2 * e_ + hf) % 3) * 1024
                    slh = pool2[:, c0:c0 + 1024]
                    srh = p2r(c0, c0 + 1024)
                    mk.emit("T", lambda e, gp=gp, k=k, g=g, slh=slh, hf=hf:
                            e.matmul(
                        pt[1][:, 1024 + gp * 512:1024 + gp * 512 + 512],
                        slh[:, (k - hf * 8) * 128:(k - hf * 8 + 1) * 128],
                        bufA[:, k * D + g * SEG:k * D + (g + 1) * SEG],
                        start=(k == 0), stop=(k == NT - 1)),
                        reads=srh + Ak(k), writes=rb, inc=(k == NT - 1))
                    if k % 2 == 1:
                        yield
                stg = pool2[:, 14336 + gp * 1024:14336 + gp * 1024 + 512]
                stgr = p2r(14336 + gp * 1024, 14336 + gp * 1024 + 512)
                mk.emit("V", lambda e, g=g, gp=gp, stg=stg:
                        e.tensor_tensor(
                    stg[:, :],
                    pt[1][:, 1024 + gp * 512:1024 + gp * 512 + 512],
                    cbbc_ap(g)[:, :], ALU.mult),
                    reads=rb + cbbc_res(g), writes=stgr + rb)
                dma("Q", d_rsp[:, e_ * D + g * SEG:e_ * D + (g + 1) * SEG],
                    stg[:, :], f"QR{gp}",
                    reads=stgr, writes=[RSP[e_][gp]])
                yield

    g3 = p3a_gen()
    g3_done = [False]
    acc3 = [0.0]
    N_MICRO = 3 + 7 + 8 * (2 + NSEG * 9)
    N_SSM = 6 * 38 - 2 * 30

    def pump3(n=1):
        for _ in range(n):
            if not g3_done[0] and next(g3, "END") == "END":
                g3_done[0] = True

    rate3 = N_MICRO / 220.0
    for w in range(6):
        for _ in ssm_wave_gen(w):
            acc3[0] += rate3
            while acc3[0] >= 1.0:
                pump3()
                acc3[0] -= 1.0
    while not g3_done[0]:
        pump3()

    if "y" in d_taps:
        dma("Q", d_taps["y"][:, :], bufB[:, :], "QTAP", reads=list(B_k))

    for c_ in range(4):
        rr = p2r(c_ * 4096, (c_ + 1) * 4096)
        dma("Q", pool2[:, c_ * 4096:(c_ + 1) * 4096],
            d_rsp[:, c_ * 4096:(c_ + 1) * 4096], f"QL{c_}",
            reads=[RSP[2 * c_][0], RSP[2 * c_][1],
                   RSP[2 * c_ + 1][0], RSP[2 * c_ + 1][1]],
            writes=rr)

    for m in range(NT):
        c0 = (m % 2) * 2048
        sl, sr = pool1[:, c0:c0 + 2048], p1r(c0, c0 + 2048)
        dma("G", sl[:, :], d_outw[m].rearrange("p k j -> p (k j)"),
            f"GS{m % 2}", reads=sr, writes=sr)
        zb = pbanks(m % 2, 0, 2048)
        for g in range(NSEG):
            mk.emit("T", lambda e, m=m, g=g: e.matmul(
                pt[m % 2][:, g * SEG:(g + 1) * SEG],
                outb_rb[0:1, m * 128:(m + 1) * 128], ones_b[0:1, 0:SEG],
                start=True, stop=False), reads=[CR("outb"), CR("ones")],
                writes=zb, inc=False)
        for k in range(NT):
            for g in range(NSEG):
                mk.emit("T", lambda e, m=m, k=k, g=g, sl=sl: e.matmul(
                    pt[m % 2][:, g * SEG:(g + 1) * SEG],
                    sl[:, k * 128:(k + 1) * 128],
                    bufB[:, k * D + g * SEG:k * D + (g + 1) * SEG],
                    start=False, stop=(k == NT - 1)),
                    reads=sr + [B_k[k]], writes=zb,
                    inc=(k == NT - 1 and g == NSEG - 1))
        inap = bass.AP(pt[m % 2], 0, [[2048, 128], [1, KC_C], [KC_C, KC_K]])
        mk.emit("V", lambda e, m=m, i=inap: e.tensor_tensor(
            bufA[:, m * D:(m + 1) * D], i, bufA[:, m * D:(m + 1) * D],
            ALU.add), reads=zb + Ak(m), writes=Ak(m) + zb)

    if "xi" in d_taps:
        dma("Q", d_taps["xi"][:, :], bufA[:, :], "QTAP", reads=[r for p in A_k2 for r in p])

    xsq = pool1[:, 0:2048]
    xsq_r = p1r(0, 2048)
    rowA = pool1[:, 2048:6144].bitcast(F32)[0:1, :]
    rowA_r = p1r(2048, 6144)
    rowB = pool1[:, 6144:10240].bitcast(F32)[0:1, :]
    rowB_r = p1r(6144, 10240)
    m_bc = pool1[:, 0:2048]
    rs_bc = pool1[:, 10240:12288]
    rs_r = p1r(10240, 12288)

    sum_b = pbanks(0, 0, 2048)
    sq_b = pbanks(1, 0, 2048)
    xsq2 = [pool1[:, 0:2048], pool1[:, 10240:12288]]
    xsq2_r = [p1r(0, 2048), p1r(10240, 12288)]

    def p3c_sum(k):
        for sgi in range(NSEG):
            mk.emit("T", lambda e, k=k, sgi=sgi: e.matmul(
                pt[0][0:1, sgi * SEG:(sgi + 1) * SEG], onescol_b[:, 0:1],
                bufA[:, k * D + sgi * SEG:k * D + (sgi + 1) * SEG],
                start=(k == 0), stop=(k == NT - 1)),
                reads=Ak(k) + [CR("onescol")], writes=sum_b,
                inc=(sgi == NSEG - 1))

    for k in range(4):
        p3c_sum(k)
    for k in range(NT):
        xq, xqr = xsq2[k % 2], xsq2_r[k % 2]
        mk.emit("S", lambda e, k=k, xq=xq: e.activation(
            xq[:, :], bufA[:, k * D:(k + 1) * D], AF.Square),
            reads=Ak(k) + xqr, writes=xqr)
        for sgi in range(NSEG):
            mk.emit("T", lambda e, k=k, sgi=sgi, xq=xq: e.matmul(
                pt[1][0:1, sgi * SEG:(sgi + 1) * SEG], onescol_b[:, 0:1],
                xq[:, sgi * SEG:(sgi + 1) * SEG],
                start=(k == 0), stop=(k == NT - 1)),
                reads=xqr + [CR("onescol")], writes=sq_b,
                inc=(sgi == NSEG - 1))
        if k + 4 < NT:
            p3c_sum(k + 4)
    mk.emit("V", lambda e: e.tensor_scalar(
        rowA[:, :], pt[0][0:1, :], 1.0 / D, None, ALU.mult),
        reads=sum_b + rowA_r, writes=rowA_r + sum_b)
    mk.emit("V", lambda e: e.tensor_copy(rowB[:, :], pt[1][0:1, :]),
            reads=sq_b + rowB_r, writes=rowB_r + sq_b)
    mk.emit("S", lambda e: e.activation(
        pt[1][0:1, :], rowA[:, :], AF.Square), reads=rowA_r, writes=sq_b)
    mk.emit("V", lambda e: e.scalar_tensor_tensor(
        rowB[:, :], rowB[:, :], 1.0 / D, pt[1][0:1, :], ALU.mult,
        ALU.subtract), reads=rowB_r + sq_b, writes=rowB_r + sq_b)
    mk.emit("S", lambda e: e.activation(
        rowB[:, :], rowB[:, :], AF.Ln, bias=EPS), reads=rowB_r + [CR("cstf")], writes=rowB_r)
    mk.emit("S", lambda e: e.activation(
        rowB[:, :], rowB[:, :], AF.Exp, scale=-0.5), reads=rowB_r + [CR("cstf")],
        writes=rowB_r)
    mk.emit("V", lambda e: e.scalar_tensor_tensor(
        rowA[:, :], rowA[:, :], -1.0, rowB[:, :], ALU.mult, ALU.mult),
        reads=rowA_r + rowB_r, writes=rowA_r)
    for src_row, srcres, dst, dres in ((rowA, rowA_r, m_bc, xsq_r),
                                       (rowB, rowB_r, rs_bc, rs_r)):
        for g in range(NSEG):
            gp = g % 2
            bb = pbanks(0, gp * 512, gp * 512 + 512)
            mk.emit("T", lambda e, src_row=src_row, g=g, gp=gp: e.matmul(
                pt[0][:, gp * 512:gp * 512 + 512], ones_f[0:1, 0:128],
                src_row[0:1, g * SEG:(g + 1) * SEG], start=True, stop=True),
                reads=srcres + [CR("onesf")], writes=bb)
            mk.emit("V", lambda e, dst=dst, g=g, gp=gp: e.tensor_copy(
                dst[:, g * SEG:(g + 1) * SEG],
                pt[0][:, gp * 512:gp * 512 + 512]), reads=bb,
                writes=dres + bb)
    def p3d_xln2(k):
        mk.emit("V", lambda e, k=k: e.tensor_tensor(
            bufB[:, k * D:(k + 1) * D], bufA[:, k * D:(k + 1) * D],
            rs_bc[:, :], ALU.mult), reads=Ak(k) + [B_k[k]] + rs_r,
            writes=[B_k[k]])
        mk.emit("V", lambda e, k=k: e.tensor_tensor(
            bufB[:, k * D:(k + 1) * D], bufB[:, k * D:(k + 1) * D],
            m_bc[:, :], ALU.add), reads=[B_k[k]] + xsq_r, writes=[B_k[k]])
        mk.emit("S", lambda e, k=k: e.activation(
            bufB[:, k * D:(k + 1) * D], bufB[:, k * D:(k + 1) * D],
            AF.Identity, bias=ln2b_c[:, k:k + 1], scale=ln2w_c[:, k:k + 1]),
            reads=[B_k[k], CR("ln2w"), CR("ln2b")], writes=[B_k[k]])

    def wb_slot(m):
        if m % 2 == 0:
            return pool1[:, 2048:3072], p1r(2048, 3072), "GS3"
        return pool1[:, 6144:7168], p1r(6144, 7168), "GS4"

    for m in range(NT):
        p3d_xln2(m)
        sl, sr, sem = wb_slot(m)
        dma("G", sl[:, :], d_wb[m].rearrange("p e j -> p (e j)"),
            sem, reads=sr, writes=sr)
        fb = pbanks(m % 2, 0, 2048)
        for e_ in range(8):
            for g in range(NSEG):
                last = (e_ == 7 and g == NSEG - 1)
                mk.emit("T", lambda e, m=m, e_=e_, g=g, sl=sl: e.matmul(
                    pt[m % 2][:, g * SEG:(g + 1) * SEG],
                    sl[:, e_ * 128:(e_ + 1) * 128],
                    pool2[:, e_ * D + g * SEG:e_ * D + (g + 1) * SEG],
                    start=(e_ == 0), stop=(e_ == 7)),
                    reads=sr + p2r(e_ * D + g * SEG, e_ * D + (g + 1) * SEG),
                    writes=fb, inc=last)
        mk.emit("V", lambda e, m=m: e.tensor_tensor(
            bufA[:, m * D:(m + 1) * D], pt[m % 2][:, :],
            bufA[:, m * D:(m + 1) * D], ALU.add),
            reads=fb + Ak(m), writes=Ak(m) + fb)

    if "xln2" in d_taps:
        dma("Q", d_taps["xln2"][:, :], bufB[:, :], "QTAP", reads=list(B_k))

    if "s3" in d_taps:
        dma("Q", d_taps["s3"][:, :], bufA[:, :], "QTAP", reads=[r for p in A_k2 for r in p])

    def w1_load(hc, mh):
        Hh = hc * 8 + mh
        c0 = (mh % 2) * 2048
        sl, sr = pool1[:, c0:c0 + 2048], p1r(c0, c0 + 2048)
        dma("G", sl[:, :], d_w1[Hh].rearrange("p k j -> p (k j)"),
            f"GS{mh % 2}", reads=sr, writes=sr)

    w1_load(0, 0)
    w1_load(0, 1)
    for hc in range(NHC):
        for mh in range(8):
            c0 = (mh % 2) * 2048
            sl, sr = pool1[:, c0:c0 + 2048], p1r(c0, c0 + 2048)
            hb = pbanks(mh % 2, 0, 2048)
            for k in range(NT):
                for g in range(NSEG):
                    last = (k == NT - 1 and g == NSEG - 1)
                    mk.emit("T", lambda e, mh=mh, k=k, g=g, sl=sl: e.matmul(
                        pt[mh % 2][:, g * SEG:(g + 1) * SEG],
                        sl[:, k * 128:(k + 1) * 128],
                        bufB[:, k * D + g * SEG:k * D + (g + 1) * SEG],
                        start=(k == 0), stop=(k == NT - 1)),
                        reads=sr + [B_k[k]], writes=hb, inc=last)
            h1res = p2r(mh * D, (mh + 1) * D)
            Hh = hc * 8 + mh
            mk.emit("S", lambda e, mh=mh, Hh=Hh: e.activation(
                pool2[:, mh * D:(mh + 1) * D], pt[mh % 2][:, :], AF.Gelu,
                bias=b1_c[:, Hh:Hh + 1]), reads=hb + [CR("b1")] + h1res,
                writes=h1res + hb)
            if mh + 2 < 8:
                w1_load(hc, mh + 2)
        for mhalf in range(2):
            for khl in range(8):
                kh = hc * 8 + khl
                c0 = 4096 + khl * 1024
                sl, sr = pool1[:, c0:c0 + 1024], p1r(c0, c0 + 1024)
                dma("G", sl[:, :],
                    d_w2[kh * 128:(kh + 1) * 128,
                         mhalf * 1024:(mhalf + 1) * 1024],
                    f"GW{khl}", reads=sr, writes=sr)
            if mhalf == 0 and hc + 1 < NHC:
                w1_load(hc + 1, 0)
                w1_load(hc + 1, 1)
            if mhalf == 1 and hc == NHC - 1:
                for t6 in range(8):
                    dma("G", bufB.bitcast(F32)[:, t6 * 2048:(t6 + 1) * 2048],
                        d_x[t6 * 128:(t6 + 1) * 128, :], "GX0",
                        reads=[B_k[2 * t6], B_k[2 * t6 + 1]],
                        writes=[B_k[2 * t6], B_k[2 * t6 + 1]])
                batch_fixup([B_k[i] for i in range(16)], "GX0")
            for ml in range(8):
                m = mhalf * 8 + ml
                mb = pbanks(ml % 2, 0, 2048)
                for khl in range(8):
                    c0 = 4096 + khl * 1024
                    sl, sr = pool1[:, c0:c0 + 1024], p1r(c0, c0 + 1024)
                    stop = (khl == 7 and hc != 0)
                    for g in range(NSEG):
                        last = (g == NSEG - 1)
                        mk.emit("T", lambda e, ml=ml, khl=khl, g=g, sl=sl,
                                stop=stop: e.matmul(
                            pt[ml % 2][:, g * SEG:(g + 1) * SEG],
                            sl[:, ml * 128:(ml + 1) * 128],
                            pool2[:, khl * D + g * SEG:khl * D + (g + 1) * SEG],
                            start=(khl == 0), stop=stop),
                            reads=sr + p2r(khl * D + g * SEG,
                                           khl * D + (g + 1) * SEG),
                            writes=mb, inc=last)
                if hc == 0:
                    for g in range(NSEG):
                        mk.emit("T", lambda e, m=m, g=g, ml=ml: e.matmul(
                            pt[ml % 2][:, g * SEG:(g + 1) * SEG],
                            b2_rb[0:1, m * 128:(m + 1) * 128],
                            ones_b[0:1, 0:SEG], start=False, stop=True),
                            reads=[CR("b2"), CR("ones")], writes=mb,
                            inc=(g == NSEG - 1))
                mk.emit("V", lambda e, m=m, ml=ml: e.tensor_tensor(
                    bufA[:, m * D:(m + 1) * D], pt[ml % 2][:, :],
                    bufA[:, m * D:(m + 1) * D], ALU.add),
                    reads=mb + Ak(m), writes=Ak(m) + mb)

    def x6_ap(t):
        if t < 8:
            return bufB.bitcast(F32)[:, t * 2048:(t + 1) * 2048]
        s = (t - 8) % 3
        return pool1[:, s * 4096:s * 4096 + 4096].bitcast(F32)

    def x6_res(t):
        if t < 8:
            return [B_k[2 * t], B_k[2 * t + 1]]
        s = (t - 8) % 3
        return p1r(s * 4096, s * 4096 + 4096)

    def p6_load(t):
        xt, xr = x6_ap(t), x6_res(t)
        sem = "GX0" if t < 8 else f"GX{1 + (t - 8) % 3}"
        dma("G", xt[:, :], d_x[t * 128:(t + 1) * 128, :], sem,
            reads=xr, writes=xr)

    for t in range(8, 11):
        p6_load(t)
    for t in range(NT):
        xt, xr = x6_ap(t), x6_res(t)
        oc0 = (t % 4) * 4096
        ot = pool2[:, oc0:oc0 + 4096].bitcast(F32)
        orr = p2r(oc0, oc0 + 4096)
        for g in range(4):
            slot = (t * 4 + g) % 4
            ptr = pt[0][:, slot * 256:slot * 256 + 256].bitcast(BF16)
            bres = RB[0][slot // 2]
            for kk in range(4):
                k = g * 4 + kk
                mk.emit("T", lambda e, ptr=ptr, kk=kk, k=k, t=t: e.transpose(
                    ptr[:, kk * 128:(kk + 1) * 128],
                    bufA[:, k * D + t * 128:k * D + t * 128 + 128],
                    ident_b[:, :]), reads=Ak(k, t // 8) + [CR("identb")],
                    writes=[bres], inc=(kk == 3))
            mk.emit("V", lambda e, ot=ot, g=g, ptr=ptr, xt=xt: e.tensor_tensor(
                ot[:, g * SEG:(g + 1) * SEG], ptr[:, :],
                xt[:, g * SEG:(g + 1) * SEG], ALU.add),
                reads=[bres] + xr + orr, writes=orr + [bres])
        dma("Q" if t < 12 else "G", d_out[t * 128:(t + 1) * 128, :],
            ot[:, :], f"QO{t % 4}" if t < 12 else f"QOG{t - 12}",
            reads=orr, writes=orr)
        if 11 <= t + 3 < NT:
            p6_load(t + 3)

    mk.wait_now("Q", ("QO0", "QO1", "QO2", "QO3", "QOG0", "QOG1",
                      "QOG2", "QOG3", "QTAP", "QC"))

    sems = {s: es.enter_context(nc.semaphore("sem" + s)) for s in ENG_SEMS}
    block = es.enter_context(nc.Block())
    mk.replay(block, sems)
    es.close()
    return nc


def host_layouts(inputs):
    g = lambda k: np.asarray(inputs[k], dtype=np.float32)
    A, Bm, Cm, Dm = g("L0_attn_A"), g("L0_attn_Bm"), g("L0_attn_Cm"), g("L0_attn_Dm")
    mats = np.stack([A.transpose(0, 2, 1), Bm.transpose(0, 2, 1),
                     Cm.transpose(0, 2, 1), Dm.transpose(0, 2, 1), A], axis=1)
    mats_p = np.ascontiguousarray(mats.transpose(2, 0, 1, 3))
    return {
        "in_w_p": np.ascontiguousarray(
            g("L0_attn_in_w").reshape(NT, 128, NT, 128).transpose(2, 1, 0, 3)),
        "out_w_p": np.ascontiguousarray(
            g("L0_attn_out_w").reshape(NT, 128, NT, 128).transpose(2, 1, 0, 3)),
        "w1_p": np.ascontiguousarray(
            g("L0_attn_w1").reshape(NT, 128, NHT, 128).transpose(2, 1, 0, 3)),
        "w2": np.ascontiguousarray(g("L0_attn_w2")),
        "wa_p": np.ascontiguousarray(
            g("L0_ffn_Wa").reshape(8, NT, 128, 128).transpose(0, 2, 1, 3)),
        "wb_p": np.ascontiguousarray(
            g("L0_ffn_Wb").reshape(8, 128, NT, 128).transpose(2, 1, 0, 3)),
        "mats_p": mats_p,
        "outb_p": np.ascontiguousarray(
            g("L0_attn_out_b").reshape(NT, 128).T),
        "router_p": np.ascontiguousarray(
            g("L0_ffn_router_w").reshape(NT, 128, 8).transpose(1, 0, 2)),
        "ln1w_p": np.ascontiguousarray(g("L0_attn_ln1_w").reshape(NT, 128).T),
        "ln1b_p": np.ascontiguousarray(g("L0_attn_ln1_b").reshape(NT, 128).T),
        "ln2w_p": np.ascontiguousarray(g("L0_attn_ln2_w").reshape(NT, 128).T),
        "ln2b_p": np.ascontiguousarray(g("L0_attn_ln2_b").reshape(NT, 128).T),
        "inb_p": np.ascontiguousarray(g("L0_attn_in_b").reshape(NT, 128).T),
        "b1_p": np.ascontiguousarray(g("L0_attn_b1").reshape(NHT, 128).T),
        "outb_r": np.ascontiguousarray(g("L0_attn_out_b").reshape(1, D)),
        "b2_r": np.ascontiguousarray(g("L0_attn_b2").reshape(1, D)),
    }


def kernel(**inputs):
    x = np.asarray(inputs["L0_attn_x"], dtype=np.float32)
    w = host_layouts(inputs)
    nc = build_program()
    in_maps = []
    for i in range(8):
        m = {"x": np.ascontiguousarray(x[i])}
        m.update(w)
        in_maps.append(m)
    res = run_bass_kernel_spmd(nc, in_maps, core_ids=list(range(8)))
    return np.stack([res.results[i]["out"] for i in range(8)], axis=0)


if __name__ == "__main__":
    build_program()
    print("build ok")


# version 45
# speedup vs baseline: 1.1956x; 1.0061x over previous
import numpy as np
import concourse.bass as bass
import concourse.mybir as mybir
import concourse.bass_utils as _bu
from concourse.bass_utils import run_bass_kernel_spmd


F32 = mybir.dt.float32
BF16 = mybir.dt.bfloat16
AF = mybir.ActivationFunctionType
ALU = mybir.AluOpType
AX = mybir.AxisListType

T_TOK = 2048
D = 2048
NT = 16
NH = 16
HIDDEN = 8192
NHT = 64
KC_K = 8
KC_C = 256
SEG = 512
NSEG = 4
NHC = 8
EPS = 1e-5

COMPUTE_SEMS = ("T", "V", "S", "G")
DMA_SEMS = ("QC", "QX0", "QX1", "QX2", "QX3", "QO0", "QO1", "QTAP",
            "GC", "GM0", "GM1", "GM2", "GM3", "GM4", "GM5", "GM6", "GM7",
            "GS0", "GS1", "GS2", "GS3", "GS4", "QR0", "QR1",
            "QA0", "QA1", "QA2", "QL0", "QL1", "QL2", "QL3",
            "GW0", "GW1", "GW2", "GW3", "GW4", "GW5", "GW6", "GW7",
            "GX0", "GX1", "GX2", "GX3", "QO2", "QO3",
            "QOG0", "QOG1", "QOG2", "QOG3")
ENG_SEMS = COMPUTE_SEMS + DMA_SEMS


class Res:
    __slots__ = ("writer", "readers", "name")

    def __init__(self, name=""):
        self.writer = None
        self.readers = []
        self.name = name


class MK:
    def __init__(self):
        self.ops = {e: [] for e in "TVSQG"}
        self.cnt = {s: 0 for s in ENG_SEMS}
        self.know = {e: {s: 0 for s in ENG_SEMS} for e in "TVSQG"}
        self.pend_reads = {e: [] for e in "TVSQG"}
        self.pend_writes = {e: [] for e in "TVSQG"}

    def emit(self, eng, fn, reads=(), writes=(), inc=True, dma_sem=None):
        waits = {}
        mysem = dma_sem if dma_sem else eng

        def need(dep):
            if dep is None:
                return
            s2, c2 = dep
            if eng == "T" and s2 == "T":
                return
            if self.know[eng].get(s2, 0) >= c2:
                return
            waits[s2] = max(waits.get(s2, 0), c2)

        for r in reads:
            need(r.writer)
        for w in writes:
            need(w.writer)
            for rd in w.readers:
                need(rd)
        for s2, c2 in waits.items():
            self.know[eng][s2] = c2

        if inc:
            self.cnt[mysem] += 1
            myc = self.cnt[mysem]
            for r in list(reads) + self.pend_reads[eng]:
                r.readers.append((mysem, myc))
            for w in list(writes) + self.pend_writes[eng]:
                w.writer = (mysem, myc)
                w.readers = []
            self.pend_reads[eng] = []
            self.pend_writes[eng] = []
        else:
            self.pend_reads[eng].extend(reads)
            self.pend_writes[eng].extend(writes)
        self.ops[eng].append((waits, fn, inc, dma_sem))
        return (mysem, self.cnt[mysem])

    def wait_now(self, eng, sems):
        waits = {s: self.cnt[s] for s in sems
                 if self.cnt[s] > self.know[eng].get(s, 0)}
        for s2, c2 in waits.items():
            self.know[eng][s2] = c2
        if waits:
            self.ops[eng].append((waits, None, False, False))

    def replay(self, block, sems):
        def make_body(ek):
            oplist = self.ops[ek]

            def body(eng):
                for waits, fn, inc, dma_sem in oplist:
                    for s2, c2 in sorted(waits.items()):
                        v = c2 * 16 if s2 in DMA_SEMS else c2
                        eng.wait_ge(sems[s2], v)
                    if fn is None:
                        continue
                    instr = fn(eng)
                    if inc:
                        s = dma_sem if dma_sem else ek
                        instr.then_inc(sems[s], 16 if dma_sem else 1)
            return body

        block.tensor(make_body("T"))
        block.vector(make_body("V"))
        block.scalar(make_body("S"))
        block.sync(make_body("Q"))
        block.gpsimd(make_body("G"))


def build_program(taps=()):
    nc = bass.Bass("TRN2", target_bir_lowering=False)
    taps = set(taps)

    d_x = nc.dram_tensor("x", [T_TOK, D], F32, kind="ExternalInput").ap()
    d_out = nc.dram_tensor("out", [T_TOK, D], F32, kind="ExternalOutput").ap()
    d_inw = nc.dram_tensor("in_w_p", [NT, 128, NT, 128], F32, kind="ExternalInput").ap()
    d_outw = nc.dram_tensor("out_w_p", [NT, 128, NT, 128], F32, kind="ExternalInput").ap()
    d_w1 = nc.dram_tensor("w1_p", [NHT, 128, NT, 128], F32, kind="ExternalInput").ap()
    d_w2 = nc.dram_tensor("w2", [HIDDEN, D], F32, kind="ExternalInput").ap()
    d_wa = nc.dram_tensor("wa_p", [8, 128, NT, 128], F32, kind="ExternalInput").ap()
    d_wb = nc.dram_tensor("wb_p", [NT, 128, 8, 128], F32, kind="ExternalInput").ap()
    d_mats = nc.dram_tensor("mats_p", [128, NH, 5, 128], F32, kind="ExternalInput").ap()
    d_rw = nc.dram_tensor("router_p", [128, NT, 8], F32, kind="ExternalInput").ap()
    d_ln1w = nc.dram_tensor("ln1w_p", [128, NT], F32, kind="ExternalInput").ap()
    d_ln1b = nc.dram_tensor("ln1b_p", [128, NT], F32, kind="ExternalInput").ap()
    d_ln2w = nc.dram_tensor("ln2w_p", [128, NT], F32, kind="ExternalInput").ap()
    d_ln2b = nc.dram_tensor("ln2b_p", [128, NT], F32, kind="ExternalInput").ap()
    d_inb = nc.dram_tensor("inb_p", [128, NT], F32, kind="ExternalInput").ap()
    d_b1 = nc.dram_tensor("b1_p", [128, NHT], F32, kind="ExternalInput").ap()
    d_outb = nc.dram_tensor("outb_r", [1, D], F32, kind="ExternalInput").ap()
    d_outbp = nc.dram_tensor("outb_p", [128, NT], F32, kind="ExternalInput").ap()
    d_b2 = nc.dram_tensor("b2_r", [1, D], F32, kind="ExternalInput").ap()

    d_rsp = nc.dram_tensor("r_spill", [128, 8 * D], BF16).ap()

    d_taps = {}
    for name in taps:
        shp = [1, 8 * T_TOK] if name == "combine" else [128, NT * D]
        d_taps[name] = nc.dram_tensor("tap_" + name, shp, BF16,
                                      kind="ExternalOutput").ap()

    from contextlib import ExitStack
    es = ExitStack()
    sbt = lambda n, sh, dt: es.enter_context(nc.sbuf_tensor(n, sh, dt))
    pst = lambda n, sh, dt: es.enter_context(nc.psum_tensor(n, sh, dt))

    bufA = sbt("bufA", [128, NT * D], BF16)
    bufB = sbt("bufB", [128, NT * D], BF16)
    pool1 = sbt("pool1", [128, 12288], BF16)
    pool2 = sbt("pool2", [128, 16384], BF16)
    ident_b = sbt("ident_b", [128, 128], BF16)
    ident_f = sbt("ident_f", [128, 128], F32)
    ones_b = sbt("ones_b", [1, SEG], BF16)
    ones_f = sbt("ones_f", [1, 128], F32)
    onescol_b = sbt("onescol_b", [128, 1], BF16)
    rwb = sbt("rwb", [128, NT * 8], BF16)
    cb8 = sbt("cb8", [8, T_TOK], BF16)
    sel8 = sbt("sel8", [8, 1024], BF16)
    ctm = sbt("ctm", [128, 128], F32)
    ctmb = sbt("ctmb", [128, 128], BF16)
    ln1w_c = sbt("ln1w_c", [128, NT], F32)
    ln1b_cb = sbt("ln1b_cb", [128, NT], BF16)
    ln2w_c = sbt("ln2w_c", [128, NT], F32)
    ln2b_c = sbt("ln2b_c", [128, NT], F32)
    inb_c = sbt("inb_c", [128, NT], F32)
    b1_c = sbt("b1_c", [128, NHT], F32)
    outb_rb = sbt("outb_rb", [1, D], BF16)
    outb_c = sbt("outb_c", [128, NT], F32)
    b2_rb = sbt("b2_rb", [1, D], BF16)
    ubias_c = sbt("ubias_c", [128, NT], F32)
    cstf = sbt("cstf", [128, 4], F32)
    stc = [sbt(f"stc{i}", [128, 8], F32) for i in range(6)]
    rtc = [sbt(f"rtc{i}", [128, 8], F32) for i in range(6)]
    ones_row128 = sbt("ones_row128", [128, 128], BF16)
    mk8 = [sbt(f"mk8_{i}", [128, 8], F32) for i in range(12)]

    pt = [pst("pt_a", [128, 2048], F32), pst("pt_b", [128, 2048], F32)]
    RB = [[Res(f"ps{t}b{b}") for b in range(4)] for t in range(2)]

    mk = MK()
    A_k2 = [[Res(f"A{k}h{h}") for h in range(2)] for k in range(NT)]

    def Ak(k, half=None):
        if half is None:
            return A_k2[k]
        return [A_k2[k][half]]

    A_k = [None] * NT
    B_k = [Res(f"B{k}") for k in range(NT)]
    P1S = [Res(f"p1_{i}") for i in range(96)]
    P2S = [Res(f"p2_{i}") for i in range(16)]
    _res = {}

    def R(name):
        if name not in _res:
            _res[name] = Res(name)
        return _res[name]

    def p1r(c0, c1):
        return [P1S[i] for i in range(c0 // 128, (c1 + 127) // 128)]

    def p2r(c0, c1):
        return [P2S[i] for i in range(c0 // 1024, (c1 + 1023) // 1024)]

    def dma(q, out, in_, sem, reads=(), writes=()):
        return mk.emit(q, lambda e, o=out, i=in_: e.dma_start(out=o, in_=i),
                       reads=reads, writes=writes, dma_sem=sem)

    def batch_fixup(res_list, sem):
        for r in res_list:
            if r.writer is not None and r.writer[0] == sem:
                r.writer = (sem, mk.cnt[sem])

    def pbanks(ti, c0, c1):
        return [RB[ti][b] for b in range(c0 // 512, (c1 + 511) // 512)]

    cres = {}
    def CR(n):
        if n not in cres:
            cres[n] = Res("c_" + n)
        return cres[n]

    for nm, dst, src in (("ln1w", ln1w_c, d_ln1w), ("ln2w", ln2w_c, d_ln2w),
                         ("ln2b", ln2b_c, d_ln2b), ("inb", inb_c, d_inb),
                         ("b1", b1_c, d_b1), ("outbc", outb_c, d_outbp)):
        dma("Q", dst[:, :], src[:, :], "QC", writes=[CR(nm)])
    dma("G", ln1b_cb[:, :], d_ln1b[:, :], "GC", writes=[CR("ln1b")])
    dma("G", rwb[:, :], d_rw.rearrange("p k e -> p (k e)"), "GC", writes=[CR("rw")])
    dma("G", outb_rb[:, :], d_outb[:, :], "GC", writes=[CR("outb")])
    dma("G", b2_rb[:, :], d_b2[:, :], "GC", writes=[CR("b2")])
    batch_fixup([CR(n) for n in ("ln1w", "ln2w", "ln2b", "inb", "b1",
                                 "outbc")], "QC")
    batch_fixup([CR(n) for n in ("ln1b", "rw", "outb", "b2")], "GC")
    mk.emit("G", lambda e: e.memset(ones_b[:, :], 1.0), writes=[CR("ones")])
    mk.emit("G", lambda e: e.memset(sel8[:, :], 1.0), writes=[CR("sel8")])
    mk.emit("G", lambda e: e.affine_select(
        out=sel8[:, :], in_=sel8[:, :], compare_op=ALU.is_ge,
        fill=0.0, base=0, pattern=[[1, 1024]], channel_multiplier=-128),
        writes=[CR("sel8")])
    mk.emit("G", lambda e: e.affine_select(
        out=sel8[:, :], in_=sel8[:, :], compare_op=ALU.is_ge,
        fill=0.0, base=127, pattern=[[-1, 1024]], channel_multiplier=128),
        writes=[CR("sel8")])
    mk.emit("G", lambda e: e.memset(onescol_b[:, :], 1.0), writes=[CR("onescol")])
    mk.emit("V", lambda e: e.memset(ones_f[:, :], 1.0), writes=[CR("onesf")])
    mk.emit("V", lambda e: e.memset(ones_row128[:, :], 1.0), writes=[CR("ones128")])
    mk.emit("V", lambda e: e.memset(cstf[:, 0:1], EPS), writes=[CR("cstf")])
    mk.emit("V", lambda e: e.memset(cstf[:, 1:2], -0.5), writes=[CR("cstf")])
    mk.emit("V", lambda e: e.memset(cstf[:, 2:3], -1.0), writes=[CR("cstf")])
    nc.const_aps.register(F32, EPS, cstf[:, 0:1])
    nc.const_aps.register(F32, -0.5, cstf[:, 1:2])
    nc.const_aps.register(F32, -1.0, cstf[:, 2:3])
    mk.emit("V", lambda e: e.memset(cstf[:, 3:4], -1.0 / D), writes=[CR("cstf")])
    nc.const_aps.register(F32, -1.0 / D, cstf[:, 3:4])
    for nm, idt in (("identb", ident_b), ("identf", ident_f)):
        mk.emit("G", lambda e, t=idt: e.memset(t[:, :], 0.0), writes=[CR(nm)])
        mk.emit("G", lambda e, t=idt: e.affine_select(
            out=t[:, :], in_=t[:, :], compare_op=ALU.not_equal, fill=1.0,
            base=0, pattern=[[-1, 128]], channel_multiplier=1),
            writes=[CR(nm)])

    NW0 = 6
    st_res = [R(f"st{i}") for i in range(NW0)]
    rCTM = R("ctm")
    rtr = [R(f"rtr{i}") for i in range(NW0)]
    rCB = R("cbfm")
    NINV = -1.0 / D

    def xtile_ap(i):
        return pool2[:, (i % 4) * 4096:(i % 4) * 4096 + 4096].bitcast(F32)

    def xtile_res(i):
        return p2r((i % 4) * 4096, (i % 4) * 4096 + 4096)

    def xntile_ap(i):
        return bufB[:, i * D:(i + 1) * D]

    def xntile_res(i):
        return [B_k[i]]

    _fill_hook = [lambda: None]

    def p0_stage(t, stage):
        xt, xnt, c = xtile_ap(t), xntile_ap(t), stc[t % NW0]
        xr, xnr, st = xtile_res(t), xntile_res(t), st_res[t % NW0]
        if stage == 0:
            dma("Q", xt[:, :], d_x[t * 128:(t + 1) * 128, :], f"QX{t % 4}",
                reads=xr, writes=xr)
        elif stage == 1:
            mk.emit("V", lambda e, xt=xt, c=c: e.reduce_sum(
                c[:, 0:1], xt[:, :], axis=AX.X), reads=xr + [st], writes=[st])
            mk.emit("S", lambda e, xt=xt, xnt=xnt, c=c: e.activation(
                xnt[:, :], xt[:, :], AF.Square, accum_out=c[:, 1:2]),
                reads=xr + [st] + xnr, writes=xnr + [st])
        elif stage == 2:
            mk.emit("V", lambda e, c=c: e.tensor_scalar(
                c[:, 2:3], c[:, 0:1], 1.0 / D, None, ALU.mult),
                reads=[st], writes=[st])
            mk.emit("V", lambda e, c=c: e.scalar_tensor_tensor(
                c[:, 2:3], c[:, 0:1], c[:, 2:3], c[:, 1:2], ALU.mult,
                ALU.subtract), reads=[st], writes=[st])
            mk.emit("V", lambda e, c=c: e.tensor_scalar(
                c[:, 5:6], c[:, 2:3], -1.0, 1e-24, ALU.mult, ALU.max),
                reads=[st], writes=[st])
        elif stage == 3:
            mk.emit("S", lambda e, c=c: e.activation(
                c[:, 3:4], c[:, 2:3], AF.Ln, bias=EPS, scale=NINV),
                reads=[st, CR("cstf")], writes=[st])
            mk.emit("S", lambda e, c=c: e.activation(
                c[:, 5:6], c[:, 5:6], AF.Ln), reads=[st], writes=[st])
            mk.emit("S", lambda e, c=c: e.activation(
                c[:, 4:5], c[:, 3:4], AF.Exp, scale=-0.5),
                reads=[st, CR("cstf")], writes=[st])
        elif stage == 4:
            mk.emit("V", lambda e, c=c: e.tensor_sub(
                c[:, 6:7], c[:, 5:6], c[:, 3:4]), reads=[st], writes=[st])
            mk.emit("V", lambda e, c=c: e.scalar_tensor_tensor(
                c[:, 7:8], c[:, 0:1], NINV, c[:, 4:5], ALU.mult, ALU.mult),
                reads=[st], writes=[st])
        elif stage == 5:
            mk.emit("S", lambda e, c=c: e.activation(
                c[:, 6:7], c[:, 6:7], AF.Exp, scale=-0.5),
                reads=[st, CR("cstf")], writes=[st])
            mk.emit("S", lambda e, xt=xt, xnt=xnt, c=c: e.activation(
                xnt[:, :], xt[:, :], AF.Identity, bias=c[:, 7:8],
                scale=c[:, 4:5]), reads=[st] + xr + xnr, writes=xnr)
        elif stage == 6:
            for g in range(4):
                if t > 0:
                    tm1 = t - 1
                    rps = pbanks(1, 1024 + (tm1 % 2) * 512,
                                 1536 + (tm1 % 2) * 512)
                    rpsum = pt[1][:, 1024 + (tm1 % 2) * 512:
                                  1024 + (tm1 % 2) * 512 + 8]
                    for kk in range(4):
                        k = g * 4 + kk
                        mk.emit("T", lambda e, k=k, tm1=tm1, rpsum=rpsum:
                                e.matmul(
                            rpsum,
                            bufA[:, k * D + tm1 * 128:k * D + tm1 * 128 + 128],
                            rwb[:, k * 8:(k + 1) * 8], start=(k == 0),
                            stop=(k == NT - 1)),
                            reads=Ak(k, tm1 // 8) + [CR("rw")], writes=rps,
                            inc=(k == NT - 1))
                if t <= 7:
                    slot = (t * 4 + g) % 8
                    ptr = pt[0][:, slot * 256:256 + slot * 256].bitcast(BF16)
                    bres = RB[0][slot // 2]
                else:
                    slot = (t * 4 + g) % 4
                    ptr = pt[0][:, 1024 + slot * 256:1280 + slot * 256].bitcast(BF16)
                    bres = RB[0][2 + slot // 2]
                for kk in range(4):
                    k = g * 4 + kk
                    mk.emit("T", lambda e, ptr=ptr, kk=kk, xnt=xnt, k=k:
                            e.transpose(ptr[:, kk * 128:(kk + 1) * 128],
                                        xnt[:, k * 128:(k + 1) * 128],
                                        ident_b[:, :]),
                            reads=xnr + [CR("identb")], writes=[bres],
                            inc=(kk == 3))
                outap = bass.AP(bufA, g * 4 * D + t * 128,
                                [[NT * D, 128], [D, 4], [1, 128]])
                mk.emit("S", lambda e, o=outap, ptr=ptr: e.activation(
                    o, ptr[:, :], AF.Copy), reads=[bres],
                    writes=[A_k2[g * 4 + i][t // 8] for i in range(4)])
                _fill_hook[0]()
        elif stage == 7:
            if t == NT - 1:
                rps = pbanks(1, 1024 + (t % 2) * 512, 1536 + (t % 2) * 512)
                rpsum = pt[1][:, 1024 + (t % 2) * 512:1024 + (t % 2) * 512 + 8]
                for k in range(NT):
                    mk.emit("T", lambda e, k=k, t=t, rpsum=rpsum: e.matmul(
                        rpsum, bufA[:, k * D + t * 128:k * D + t * 128 + 128],
                        rwb[:, k * 8:(k + 1) * 8], start=(k == 0),
                        stop=(k == NT - 1)),
                        reads=Ak(k, t // 8) + [CR("rw")], writes=rps,
                        inc=(k == NT - 1))
            _fill_hook[0]()
        elif stage == 8:
            rps = pbanks(1, 1024 + (t % 2) * 512, 1536 + (t % 2) * 512)
            rpsum = pt[1][:, 1024 + (t % 2) * 512:1024 + (t % 2) * 512 + 8]
            lg, msk1 = mk8[(t % NW0) * 2], mk8[(t % NW0) * 2 + 1]
            rt, rc = rtr[t % NW0], rtc[t % NW0]
            mk.emit("V", lambda e, c=c, lg=lg, rpsum=rpsum: e.tensor_scalar_mul(
                lg[:, :], rpsum, c[:, 6:7]),
                reads=rps + [st, rt], writes=[rt] + rps)
            mk.emit("V", lambda e, lg=lg, rc=rc: e.reduce_max(
                rc[:, 0:1], lg[:, :], axis=AX.X, negate=True),
                reads=[rt], writes=[rt])
            mk.emit("V", lambda e, lg=lg, rc=rc, msk1=msk1: e.tensor_scalar(
                msk1[:, :], lg[:, :], rc[:, 0:1], 0.0, ALU.add, ALU.is_ge),
                reads=[rt], writes=[rt])
            mk.emit("V", lambda e, lg=lg, msk1=msk1: e.scalar_tensor_tensor(
                lg[:, :], msk1[:, :], -2e30, lg[:, :], ALU.mult, ALU.add),
                reads=[rt], writes=[rt])
            mk.emit("V", lambda e, lg=lg, rc=rc: e.reduce_max(
                rc[:, 1:2], lg[:, :], axis=AX.X, negate=True),
                reads=[rt], writes=[rt])
            mk.emit("V", lambda e, lg=lg, rc=rc: e.tensor_scalar(
                lg[:, :], lg[:, :], rc[:, 1:2], 0.0, ALU.add, ALU.is_ge),
                reads=[rt], writes=[rt])
            mk.emit("S", lambda e, rc=rc: e.activation(
                rc[:, 2:3], rc[:, 1:2], AF.Exp, bias=rc[:, 0:1], scale=-1.0),
                reads=[rt, CR("cstf")], writes=[rt])
            mk.emit("V", lambda e, rc=rc: e.tensor_scalar(
                rc[:, 3:4], rc[:, 2:3], 1.0, None, ALU.add),
                reads=[rt], writes=[rt])
            mk.emit("V", lambda e, rc=rc: e.reciprocal(
                rc[:, 4:5], rc[:, 3:4]), reads=[rt], writes=[rt])
            mk.emit("V", lambda e, lg=lg, msk1=msk1: e.tensor_sub(
                msk1[:, :], msk1[:, :], lg[:, :]), reads=[rt], writes=[rt])
            mk.emit("V", lambda e, t=t, lg=lg, msk1=msk1, rc=rc:
                    e.scalar_tensor_tensor(
                ctm[:, t * 8:(t + 1) * 8], msk1[:, :], rc[:, 4:5], lg[:, :],
                ALU.mult, ALU.add), reads=[rt], writes=[rCTM, rt])

    steps = {}
    for t in range(NT):
        for stage in range(9):
            off = {6: 1, 7: 1, 8: 2}.get(stage, 0)
            steps.setdefault((3 * t) // 2 + stage + off,
                             []).append((t, stage))
    p0_order = sorted(steps)
    p0_head = [s for s in p0_order if s <= 18]
    p0_tail = [s for s in p0_order if s > 18]

    if "xn" in d_taps:
        pass

    rUB = R("ubias")
    ln1w_full = pool1[:, 6144:8192]
    lwf_r = p1r(6144, 8192)

    def p1_pre_gen():
        for k in range(NT):
            mk.emit("V", lambda e, k=k: e.tensor_scalar(
                ln1w_full[:, k * 128:(k + 1) * 128], ones_row128[:, :],
                ln1w_c[:, k:k + 1], None, ALU.mult),
                reads=[CR("ln1w"), CR("ones128")] + lwf_r, writes=lwf_r,
                inc=(k == NT - 1))

        def pload(m):
            c0 = 8192 + (m % 2) * 2048
            sl, sr = pool1[:, c0:c0 + 2048], p1r(c0, c0 + 2048)
            dma("G", sl[:, :], d_inw[m].rearrange("p k j -> p (k j)"),
                f"GS{3 + m % 2}", reads=sr, writes=sr)

        pload(0)
        pload(1)
        yield
        for m in range(NT):
            c0 = 8192 + (m % 2) * 2048
            sl, sr = pool1[:, c0:c0 + 2048], p1r(c0, c0 + 2048)
            bb = pbanks(1, 512, 1024)
            for k in range(NT):
                mk.emit("T", lambda e, k=k, sl=sl: e.matmul(
                    pt[1][:, 512:513], sl[:, k * 128:(k + 1) * 128],
                    ln1b_cb[:, k:k + 1], start=(k == 0), stop=(k == NT - 1)),
                    reads=sr + [CR("ln1b")], writes=bb, inc=(k == NT - 1))
            mk.emit("V", lambda e, m=m: e.tensor_tensor(
                ubias_c[:, m:m + 1], pt[1][:, 512:513],
                inb_c[:, m:m + 1], ALU.add), reads=bb + [CR("inb")],
                writes=[rUB] + bb)
            if m + 2 < NT:
                pload(m + 2)
            yield

    def p1_load(m):
        c0 = (m % 3) * 2048
        sl, sr = pool1[:, c0:c0 + 2048], p1r(c0, c0 + 2048)
        dma("G", sl[:, :], d_inw[m].rearrange("p k j -> p (k j)"),
            f"GS{m % 3}", reads=sr, writes=sr)

    def p1_prep(m):
        c0 = (m % 3) * 2048
        sl, sr = pool1[:, c0:c0 + 2048], p1r(c0, c0 + 2048)
        mk.emit("V", lambda e, sl=sl: e.tensor_tensor(
            sl[:, :], sl[:, :], ln1w_full[:, :], ALU.mult),
            reads=sr + lwf_r, writes=sr)

    def p1_mms(m, half, ks):
        c0 = (m % 3) * 2048
        sl, sr = pool1[:, c0:c0 + 2048], p1r(c0, c0 + 2048)
        ub = pbanks(m % 2, 0, 1024)
        for k in ks:
            for gg in range(2):
                g = half * 2 + gg
                last = (k == NT - 1 and gg == 1)
                mk.emit("T", lambda e, m=m, k=k, g=g, gg=gg, sl=sl: e.matmul(
                    pt[m % 2][:, gg * 512:(gg + 1) * 512],
                    sl[:, k * 128:(k + 1) * 128],
                    bufA[:, k * D + g * SEG:k * D + (g + 1) * SEG],
                    start=(k == 0), stop=(k == NT - 1)),
                    reads=sr + Ak(k, half), writes=ub, inc=last)

    def p1_evac(m, half):
        ub = pbanks(m % 2, 0, 1024)
        outap = bass.AP(bufB, m * D + half * 128,
                        [[NT * D, 128], [1, 128], [KC_C, KC_K]])
        mk.emit("S", lambda e, m=m, o=outap: e.activation(
            o, pt[m % 2][:, 0:1024], AF.Identity, bias=ubias_c[:, m:m + 1]),
            reads=ub + [rUB], writes=[B_k[m]] + ub)

    def p1_sweep(half):
        p1_load(0)
        p1_load(1)
        p1_load(2)
        p1_prep(0)
        for m in range(NT):
            p1_mms(m, half, range(8))
            yield
            if m + 1 < NT:
                p1_prep(m + 1)
            p1_mms(m, half, range(8, NT))
            yield
            if m + 3 < NT:
                p1_load(m + 3)
            p1_evac(m, half)
            yield

    _g_pre = [None]

    def pump_pre():
        if _g_pre[0] is not None:
            if next(_g_pre[0], "END") == "END":
                _g_pre[0] = None

    for sidx in p0_head:
        for t, stage in steps[sidx]:
            p0_stage(t, stage)
        if sidx == 1:
            _g_pre[0] = p1_pre_gen()
        pump_pre()
    while _g_pre[0] is not None:
        pump_pre()

    gA = p1_sweep(0)
    ti = 0
    for _ in gA:
        for _ in range(2):
            if ti < len(p0_tail):
                for t, stage in steps[p0_tail[ti]]:
                    p0_stage(t, stage)
                ti += 1
    while ti < len(p0_tail):
        for t, stage in steps[p0_tail[ti]]:
            p0_stage(t, stage)
        ti += 1
    if "xn" in d_taps:
        dma("Q", d_taps["xn"][:, :], bufA[:, :], "QTAP",
            reads=[r for p in A_k2 for r in p])
    for _ in p1_sweep(1):
        pass

    if "u" in d_taps:
        dma("Q", d_taps["u"][:, :], bufB[:, :], "QTAP", reads=list(B_k))

    mats_ap = lambda h: pool1[:, (h % 8) * 640:(h % 8) * 640 + 640]
    mats_res = lambda h: p1r((h % 8) * 640, (h % 8) * 640 + 640)
    xxt_ap = lambda h: pool1[:, 5120 + (h % 3) * 512:5120 + (h % 3) * 512 + 512]
    xxt_res = lambda h: p1r(5120 + (h % 3) * 512, 5120 + (h % 3) * 512 + 512)
    P_ap = lambda h: pool1[:, 6656 + (h % 3) * 512:6656 + (h % 3) * 512 + 512]
    P_res = lambda h: p1r(6656 + (h % 3) * 512, 6656 + (h % 3) * 512 + 512)
    H_ap = lambda h: pool1[:, 8192 + (h % 3) * 256:8192 + (h % 3) * 256 + 256]
    H_res = lambda h: p1r(8192 + (h % 3) * 256, 8192 + (h % 3) * 256 + 256)
    G_ap = lambda h: pool1[:, 8960 + (h % 3) * 512:8960 + (h % 3) * 512 + 512]
    G_res = lambda h: p1r(8960 + (h % 3) * 512, 8960 + (h % 3) * 512 + 512)
    L_ap = lambda h: pool2[:, (h % 3) * 2048:(h % 3) * 2048 + 2048]
    L_res = lambda h: p2r((h % 3) * 2048, (h % 3) * 2048 + 2048)
    powT_ap = lambda h: pool2[:, 6144 + (h % 3) * 1024:6144 + (h % 3) * 1024 + 1024]
    powT_res = lambda h: p2r(6144 + (h % 3) * 1024, 6144 + (h % 3) * 1024 + 1024)
    _SLOT3 = [(0, 0), (0, 1024), (1, 0)]

    def sten(h):
        return pt[_SLOT3[h % 3][0]]

    def sbase(h):
        return _SLOT3[h % 3][1]

    def spb(h, lo, hi):
        return pbanks(_SLOT3[h % 3][0], sbase(h) + lo, sbase(h) + hi)

    def mats_dma(h):
        mat, mres = mats_ap(h), mats_res(h)
        dma("G", mat[:, :], d_mats[:, h], f"GM{h % 8}", reads=mres, writes=mres)

    _prep = {}

    def prep_level(h, lvl):
        p10, base = sten(h), sbase(h)
        mat, xxt, powt = mats_ap(h), xxt_ap(h), powT_ap(h)
        mres, xres, pwres = mats_res(h), xxt_res(h), powT_res(h)
        if lvl == 0:
            _prep[h] = (mat[:, 512:640], mat[:, 0:128], mres)
        curX, curXT, curres = _prep[h]
        pba = spb(h, 768, 896)
        pbb = spb(h, 896, 1024)
        mk.emit("T", lambda e, XT=curXT, X=curX, p10=p10, base=base: e.matmul(
            p10[:, base + 768:base + 896], XT, X, start=True, stop=True),
            reads=curres, writes=pba)
        mk.emit("T", lambda e, XT=curXT, X=curX, p10=p10, base=base: e.matmul(
            p10[:, base + 896:base + 1024], X, XT, start=True, stop=True),
            reads=curres, writes=pbb)
        nX = xxt[:, (lvl % 2) * 256:(lvl % 2) * 256 + 128]
        if lvl >= 2:
            nXT = powt[:, (lvl - 2) * 128:(lvl - 1) * 128]
            nXTr = pwres
        else:
            nXT = xxt[:, (lvl % 2) * 256 + 128:(lvl % 2) * 256 + 256]
            nXTr = xres
        mk.emit("V", lambda e, nX=nX, p10=p10, base=base: e.tensor_copy(
            nX, p10[:, base + 768:base + 896]), reads=pba, writes=xres + pba)
        mk.emit("S", lambda e, nXT=nXT, p10=p10, base=base: e.activation(
            nXT, p10[:, base + 896:base + 1024], AF.Copy), reads=pbb,
            writes=nXTr + pbb)
        _prep[h] = (nX, nXT, xres + (pwres if lvl >= 2 else []))

    def ph1_step(h, tau):
        p10, base = sten(h), sbase(h)
        mat, L = mats_ap(h), L_ap(h)
        mres = mats_res(h)
        vb = spb(h, 0, 256)
        mk.emit("T", lambda e, h=h, tau=tau, mat=mat, p10=p10, base=base:
                e.matmul(p10[:, base:base + 256], mat[:, 128:256],
                         bufB[:, h * D + tau * KC_C:h * D + (tau + 1) * KC_C],
                         start=True, stop=(tau == 0)),
                reads=mres + [B_k[h]], writes=vb, inc=(tau == 0))
        if tau > 0:
            mk.emit("T", lambda e, tau=tau, mat=mat, L=L, p10=p10, base=base:
                    e.matmul(p10[:, base:base + 256], mat[:, 0:128],
                             L[:, (tau - 1) * KC_C:tau * KC_C],
                             start=False, stop=True),
                    reads=mres + L_res(h), writes=vb)
        mk.emit("S", lambda e, tau=tau, L=L, p10=p10, base=base: e.activation(
            L[:, tau * KC_C:(tau + 1) * KC_C], p10[:, base:base + 256],
            AF.Copy), reads=vb, writes=L_res(h) + vb)

    _ph2 = {}

    def ph2_step(h, r):
        p10, base = sten(h), sbase(h)
        L, powt, Pb = L_ap(h), powT_ap(h), P_ap(h)
        if r == 1:
            _ph2[h] = (L[:, (KC_K - 1) * KC_C:KC_K * KC_C], L_res(h))
        src2, srcres = _ph2[h]
        d_ = 1 << (r - 1)
        hb = spb(h, 256, 512)
        mk.emit("T", lambda e, r=r, src2=src2, powt=powt, d_=d_, p10=p10,
                base=base: e.matmul(
            p10[:, base + 256:base + 256 + KC_C - d_],
            powt[:, (r - 1) * 128:r * 128], src2[:, 0:KC_C - d_],
            start=True, stop=True), reads=srcres + powT_res(h), writes=hb)
        dst = Pb[:, (r % 2) * 256:(r % 2) * 256 + 256]
        mk.emit("V", lambda e, dst=dst, src2=src2, d_=d_, p10=p10, base=base:
                e.tensor_tensor(dst[:, d_:KC_C],
                                p10[:, base + 256:base + 256 + KC_C - d_],
                                src2[:, d_:KC_C], ALU.add),
                reads=hb + srcres, writes=P_res(h) + hb)
        mk.emit("S", lambda e, dst=dst, src2=src2, d_=d_: e.activation(
            dst[:, 0:d_], src2[:, 0:d_], AF.Copy), reads=srcres,
            writes=P_res(h))
        _ph2[h] = (dst, P_res(h))

    def ph2_fin(h):
        src2, srcres = _ph2[h]
        Hb, hres = H_ap(h), H_res(h)
        mk.emit("V", lambda e, Hb=Hb: e.memset(Hb[:, 0:1], 0.0),
                reads=hres, writes=hres)
        mk.emit("V", lambda e, Hb=Hb, src2=src2: e.tensor_copy(
            Hb[:, 1:KC_C], src2[:, 0:KC_C - 1]), reads=srcres + hres,
            writes=hres)
        _ph2[h] = (Hb, hres)

    _ph3 = {}

    def ph3_step(h, tau):
        p10, base = sten(h), sbase(h)
        mat, L, G = mats_ap(h), L_ap(h), G_ap(h)
        if tau == 0:
            _ph3[h] = _ph2[h]
        prev, prevres = _ph3[h]
        gb = spb(h, 512, 768)
        mk.emit("T", lambda e, mat=mat, prev=prev, p10=p10, base=base:
                e.matmul(p10[:, base + 512:base + 768], mat[:, 0:128],
                         prev[:, 0:KC_C], start=True, stop=True),
                reads=mats_res(h) + prevres, writes=gb)
        gdst = G[:, (tau % 2) * 256:(tau % 2) * 256 + 256]
        mk.emit("S", lambda e, gdst=gdst, p10=p10, base=base: e.activation(
            gdst[:, :], p10[:, base + 512:base + 768], AF.Copy), reads=gb,
            writes=G_res(h) + gb)
        mk.emit("V", lambda e, tau=tau, L=L, gdst=gdst: e.tensor_tensor(
            L[:, tau * KC_C:(tau + 1) * KC_C],
            L[:, tau * KC_C:(tau + 1) * KC_C], gdst[:, :], ALU.add),
            reads=G_res(h) + L_res(h), writes=L_res(h))
        _ph3[h] = (gdst, G_res(h))

    def ph4_seg(h, g):
        p10, base = sten(h), sbase(h)
        mat, L = mats_ap(h), L_ap(h)
        yb = spb(h, 0, 512)
        mk.emit("T", lambda e, mat=mat, L=L, g=g, p10=p10, base=base:
                e.matmul(p10[:, base:base + 512], mat[:, 256:384],
                         L[:, g * 512:(g + 1) * 512], start=True, stop=False),
                reads=mats_res(h) + L_res(h), writes=yb, inc=False)
        mk.emit("T", lambda e, mat=mat, h=h, g=g, p10=p10, base=base:
                e.matmul(p10[:, base:base + 512], mat[:, 384:512],
                         bufB[:, h * D + g * 512:h * D + (g + 1) * 512],
                         start=False, stop=True),
                reads=mats_res(h) + [B_k[h]], writes=yb)
        mk.emit("V", lambda e, h=h, g=g, p10=p10, base=base: e.tensor_copy(
            bufB[:, h * D + g * 512:h * D + (g + 1) * 512],
            p10[:, base:base + 512]), reads=yb, writes=[B_k[h]] + yb)

    def ssm_wave_gen(w):
        wave = [h for h in range(3 * w, min(3 * w + 3, NH))]
        if w == 0:
            for h in range(8):
                mats_dma(h)
        elif 3 * w + 3 <= NH:
            for h in wave:
                if 8 <= h + 3 < NH:
                    mats_dma(h + 3)
        for lvl in range(10):
            for h in wave:
                prep_level(h, lvl)
            yield
        for tau in range(KC_K):
            for h in wave:
                ph1_step(h, tau)
            yield
        for r in range(1, 9):
            for h in wave:
                ph2_step(h, r)
            yield
        for h in wave:
            ph2_fin(h)
        for tau in range(KC_K):
            for h in wave:
                ph3_step(h, tau)
            yield
        for g in range(NSEG):
            for h in wave:
                ph4_seg(h, g)
            yield

    rCTMB = R("ctmb")
    RSP = [[R(f"rsp{i}_{p}") for p in range(2)] for i in range(8)]

    def wa_load(e_, hf):
        c0 = 9216 + ((2 * e_ + hf) % 3) * 1024
        slh, srh = pool2[:, c0:c0 + 1024], p2r(c0, c0 + 1024)
        dma("G", slh[:, :],
            d_wa[e_, :, hf * 8:(hf + 1) * 8, :].rearrange("p k j -> p (k j)"),
            f"QA{(2 * e_ + hf) % 3}", reads=srh, writes=srh)

    def cbbc_ap(g):
        return pool2[:, 12288 + g * 512:12288 + g * 512 + 512]

    def cbbc_res(g):
        return p2r(12288 + g * 512, 12288 + g * 512 + 512)

    def p3a_gen():
        mk.emit("V", lambda e: e.tensor_copy(ctmb[:, :], ctm[:, :]),
                reads=[rCTM], writes=[rCTMB])
        wa_load(0, 0)
        wa_load(0, 1)
        wa_load(1, 0)
        yield
        rowb = pbanks(1, 1024, 2048)
        for rnd in range(2):
            for tt in range(8):
                t = rnd * 8 + tt
                mk.emit("T", lambda e, t=t, tt=tt: e.matmul(
                    pt[1][0:8, 1024 + tt * 128:1024 + (tt + 1) * 128],
                    ctmb[:, t * 8:(t + 1) * 8], ident_b[:, :],
                    start=True, stop=True),
                    reads=[rCTMB, CR("identb")], writes=rowb,
                    inc=(tt == 7))
            mk.emit("V", lambda e, rnd=rnd: e.tensor_copy(
                cb8[0:8, rnd * 1024:(rnd + 1) * 1024],
                pt[1][0:8, 1024:2048]), reads=rowb, writes=[rCB] + rowb)
            yield
        for e_ in range(8):
            if e_ >= 1:
                wa_load(e_, 1)
                if e_ < 7:
                    wa_load(e_ + 1, 0)
                yield
            for gh in range(2):
                for g in (gh, gh + 2):
                    bb = pbanks(1, 1024 + (g % 2) * 512, 1536 + (g % 2) * 512)
                    mk.emit("T", lambda e, e_=e_, g=g: e.matmul(
                        pt[1][:, 1024 + (g % 2) * 512:1536 + (g % 2) * 512],
                        sel8[0:8, e_ * 128:(e_ + 1) * 128],
                        cb8[0:8, g * SEG:(g + 1) * SEG],
                        start=True, stop=True), reads=[rCB, CR("sel8")],
                        writes=bb)
                    mk.emit("S", lambda e, g=g: e.activation(
                        cbbc_ap(g)[:, :],
                        pt[1][:, 1024 + (g % 2) * 512:1536 + (g % 2) * 512],
                        AF.Copy), reads=bb, writes=cbbc_res(g) + bb)
                yield
            for g in range(NSEG):
                gp = g % 2
                rb = pbanks(1, 1024 + gp * 512, 1536 + gp * 512)
                for k in range(NT):
                    hf = k // 8
                    c0 = 9216 + ((2 * e_ + hf) % 3) * 1024
                    slh = pool2[:, c0:c0 + 1024]
                    srh = p2r(c0, c0 + 1024)
                    mk.emit("T", lambda e, gp=gp, k=k, g=g, slh=slh, hf=hf:
                            e.matmul(
                        pt[1][:, 1024 + gp * 512:1024 + gp * 512 + 512],
                        slh[:, (k - hf * 8) * 128:(k - hf * 8 + 1) * 128],
                        bufA[:, k * D + g * SEG:k * D + (g + 1) * SEG],
                        start=(k == 0), stop=(k == NT - 1)),
                        reads=srh + Ak(k), writes=rb, inc=(k == NT - 1))
                    if k % 2 == 1:
                        yield
                stg = pool2[:, 14336 + gp * 1024:14336 + gp * 1024 + 512]
                stgr = p2r(14336 + gp * 1024, 14336 + gp * 1024 + 512)
                mk.emit("V", lambda e, g=g, gp=gp, stg=stg:
                        e.tensor_tensor(
                    stg[:, :],
                    pt[1][:, 1024 + gp * 512:1024 + gp * 512 + 512],
                    cbbc_ap(g)[:, :], ALU.mult),
                    reads=rb + cbbc_res(g), writes=stgr + rb)
                dma("Q", d_rsp[:, e_ * D + g * SEG:e_ * D + (g + 1) * SEG],
                    stg[:, :], f"QR{gp}",
                    reads=stgr, writes=[RSP[e_][gp]])
                yield

    g3 = p3a_gen()
    g3_done = [False]
    acc3 = [0.0]
    N_MICRO = 3 + 7 + 8 * (2 + NSEG * 9)
    N_SSM = 6 * 38 - 2 * 30

    def pump3(n=1):
        for _ in range(n):
            if not g3_done[0] and next(g3, "END") == "END":
                g3_done[0] = True

    rate3 = N_MICRO / 220.0
    for w in range(6):
        for _ in ssm_wave_gen(w):
            acc3[0] += rate3
            while acc3[0] >= 1.0:
                pump3()
                acc3[0] -= 1.0
    while not g3_done[0]:
        pump3()

    if "y" in d_taps:
        dma("Q", d_taps["y"][:, :], bufB[:, :], "QTAP", reads=list(B_k))

    for c_ in range(4):
        rr = p2r(c_ * 4096, (c_ + 1) * 4096)
        dma("Q", pool2[:, c_ * 4096:(c_ + 1) * 4096],
            d_rsp[:, c_ * 4096:(c_ + 1) * 4096], f"QL{c_}",
            reads=[RSP[2 * c_][0], RSP[2 * c_][1],
                   RSP[2 * c_ + 1][0], RSP[2 * c_ + 1][1]],
            writes=rr)

    for m in range(NT):
        c0 = (m % 2) * 2048
        sl, sr = pool1[:, c0:c0 + 2048], p1r(c0, c0 + 2048)
        dma("G", sl[:, :], d_outw[m].rearrange("p k j -> p (k j)"),
            f"GS{m % 2}", reads=sr, writes=sr)
        zb = pbanks(m % 2, 0, 2048)
        for g in range(NSEG):
            mk.emit("T", lambda e, m=m, g=g: e.matmul(
                pt[m % 2][:, g * SEG:(g + 1) * SEG],
                outb_rb[0:1, m * 128:(m + 1) * 128], ones_b[0:1, 0:SEG],
                start=True, stop=False), reads=[CR("outb"), CR("ones")],
                writes=zb, inc=False)
        for k in range(NT):
            for g in range(NSEG):
                mk.emit("T", lambda e, m=m, k=k, g=g, sl=sl: e.matmul(
                    pt[m % 2][:, g * SEG:(g + 1) * SEG],
                    sl[:, k * 128:(k + 1) * 128],
                    bufB[:, k * D + g * SEG:k * D + (g + 1) * SEG],
                    start=False, stop=(k == NT - 1)),
                    reads=sr + [B_k[k]], writes=zb,
                    inc=(k == NT - 1 and g == NSEG - 1))
        inap = bass.AP(pt[m % 2], 0, [[2048, 128], [1, KC_C], [KC_C, KC_K]])
        mk.emit("V", lambda e, m=m, i=inap: e.tensor_tensor(
            bufA[:, m * D:(m + 1) * D], i, bufA[:, m * D:(m + 1) * D],
            ALU.add), reads=zb + Ak(m), writes=Ak(m) + zb)

    if "xi" in d_taps:
        dma("Q", d_taps["xi"][:, :], bufA[:, :], "QTAP", reads=[r for p in A_k2 for r in p])

    xsq = pool1[:, 0:2048]
    xsq_r = p1r(0, 2048)
    rowA = bufB.bitcast(F32)[0:1, 12288:14336]
    rowA_r = [B_k[12], B_k[13]]
    rowB = bufB.bitcast(F32)[0:1, 14336:16384]
    rowB_r = [B_k[14], B_k[15]]
    m_bc = pool1[:, 0:2048]
    rs_bc = pool1[:, 10240:12288]
    rs_r = p1r(10240, 12288)

    sum_b = pbanks(0, 0, 2048)
    sq_b = pbanks(1, 0, 2048)
    xsq2 = [pool1[:, 0:2048], pool1[:, 10240:12288]]
    xsq2_r = [p1r(0, 2048), p1r(10240, 12288)]

    def p3c_sum(k):
        for sgi in range(NSEG):
            mk.emit("T", lambda e, k=k, sgi=sgi: e.matmul(
                pt[0][0:1, sgi * SEG:(sgi + 1) * SEG], onescol_b[:, 0:1],
                bufA[:, k * D + sgi * SEG:k * D + (sgi + 1) * SEG],
                start=(k == 0), stop=(k == NT - 1)),
                reads=Ak(k) + [CR("onescol")], writes=sum_b,
                inc=(sgi == NSEG - 1))

    for k in range(4):
        p3c_sum(k)
    for k in range(NT):
        xq, xqr = xsq2[k % 2], xsq2_r[k % 2]
        mk.emit("S", lambda e, k=k, xq=xq: e.activation(
            xq[:, :], bufA[:, k * D:(k + 1) * D], AF.Square),
            reads=Ak(k) + xqr, writes=xqr)
        for sgi in range(NSEG):
            mk.emit("T", lambda e, k=k, sgi=sgi, xq=xq: e.matmul(
                pt[1][0:1, sgi * SEG:(sgi + 1) * SEG], onescol_b[:, 0:1],
                xq[:, sgi * SEG:(sgi + 1) * SEG],
                start=(k == 0), stop=(k == NT - 1)),
                reads=xqr + [CR("onescol")], writes=sq_b,
                inc=(sgi == NSEG - 1))
        if k + 4 < NT:
            p3c_sum(k + 4)
    mk.emit("V", lambda e: e.tensor_scalar(
        rowA[:, :], pt[0][0:1, :], 1.0 / D, None, ALU.mult),
        reads=sum_b + rowA_r, writes=rowA_r + sum_b)
    mk.emit("V", lambda e: e.tensor_copy(rowB[:, :], pt[1][0:1, :]),
            reads=sq_b + rowB_r, writes=rowB_r + sq_b)
    mk.emit("S", lambda e: e.activation(
        pt[1][0:1, :], rowA[:, :], AF.Square), reads=rowA_r, writes=sq_b)
    mk.emit("V", lambda e: e.scalar_tensor_tensor(
        rowB[:, :], rowB[:, :], 1.0 / D, pt[1][0:1, :], ALU.mult,
        ALU.subtract), reads=rowB_r + sq_b, writes=rowB_r + sq_b)
    mk.emit("S", lambda e: e.activation(
        rowB[:, :], rowB[:, :], AF.Ln, bias=EPS), reads=rowB_r + [CR("cstf")], writes=rowB_r)
    mk.emit("S", lambda e: e.activation(
        rowB[:, :], rowB[:, :], AF.Exp, scale=-0.5), reads=rowB_r + [CR("cstf")],
        writes=rowB_r)
    mk.emit("V", lambda e: e.scalar_tensor_tensor(
        rowA[:, :], rowA[:, :], -1.0, rowB[:, :], ALU.mult, ALU.mult),
        reads=rowA_r + rowB_r, writes=rowA_r)
    def p3d_xln2(k):
        mk.emit("V", lambda e, k=k: e.tensor_tensor(
            bufB[:, k * D:(k + 1) * D], bufA[:, k * D:(k + 1) * D],
            rs_bc[:, :], ALU.mult), reads=Ak(k) + [B_k[k]] + rs_r,
            writes=[B_k[k]])
        mk.emit("V", lambda e, k=k: e.tensor_tensor(
            bufB[:, k * D:(k + 1) * D], bufB[:, k * D:(k + 1) * D],
            m_bc[:, :], ALU.add), reads=[B_k[k]] + xsq_r, writes=[B_k[k]])
        mk.emit("S", lambda e, k=k: e.activation(
            bufB[:, k * D:(k + 1) * D], bufB[:, k * D:(k + 1) * D],
            AF.Identity, bias=ln2b_c[:, k:k + 1], scale=ln2w_c[:, k:k + 1]),
            reads=[B_k[k], CR("ln2w"), CR("ln2b")], writes=[B_k[k]])

    def wb_slot(m):
        if m % 2 == 0:
            return pool1[:, 2048:3072], p1r(2048, 3072), "GS3"
        return pool1[:, 6144:7168], p1r(6144, 7168), "GS4"

    def p3e_mms(m):
        sl, sr, sem = wb_slot(m)
        dma("G", sl[:, :], d_wb[m].rearrange("p e j -> p (e j)"),
            sem, reads=sr, writes=sr)
        fb = pbanks(m % 2, 0, 2048)
        for e_ in range(8):
            for g in range(NSEG):
                last = (e_ == 7 and g == NSEG - 1)
                mk.emit("T", lambda e, m=m, e_=e_, g=g, sl=sl: e.matmul(
                    pt[m % 2][:, g * SEG:(g + 1) * SEG],
                    sl[:, e_ * 128:(e_ + 1) * 128],
                    pool2[:, e_ * D + g * SEG:e_ * D + (g + 1) * SEG],
                    start=(e_ == 0), stop=(e_ == 7)),
                    reads=sr + p2r(e_ * D + g * SEG, e_ * D + (g + 1) * SEG),
                    writes=fb, inc=last)

    def p3e_evac(m):
        fb = pbanks(m % 2, 0, 2048)
        mk.emit("V", lambda e, m=m: e.tensor_tensor(
            bufA[:, m * D:(m + 1) * D], pt[m % 2][:, :],
            bufA[:, m * D:(m + 1) * D], ALU.add),
            reads=fb + Ak(m), writes=Ak(m) + fb)

    p3e_mms(0)
    for src_row, srcres, dst, dres in ((rowA, rowA_r, m_bc, xsq_r),
                                       (rowB, rowB_r, rs_bc, rs_r)):
        for g in range(NSEG):
            gp = g % 2
            bb = pbanks(1, gp * 512, gp * 512 + 512)
            mk.emit("T", lambda e, src_row=src_row, g=g, gp=gp: e.matmul(
                pt[1][:, gp * 512:gp * 512 + 512], ones_f[0:1, 0:128],
                src_row[0:1, g * SEG:(g + 1) * SEG], start=True, stop=True),
                reads=srcres + [CR("onesf")], writes=bb)
            mk.emit("V", lambda e, dst=dst, g=g, gp=gp: e.tensor_copy(
                dst[:, g * SEG:(g + 1) * SEG],
                pt[1][:, gp * 512:gp * 512 + 512]), reads=bb,
                writes=dres + bb)
    p3d_xln2(0)
    p3e_evac(0)
    for m in range(1, NT):
        p3d_xln2(m)
        p3e_mms(m)
        p3e_evac(m)

    if "xln2" in d_taps:
        dma("Q", d_taps["xln2"][:, :], bufB[:, :], "QTAP", reads=list(B_k))

    if "s3" in d_taps:
        dma("Q", d_taps["s3"][:, :], bufA[:, :], "QTAP", reads=[r for p in A_k2 for r in p])

    def w1_load(hc, mh):
        Hh = hc * 8 + mh
        c0 = (mh % 2) * 2048
        sl, sr = pool1[:, c0:c0 + 2048], p1r(c0, c0 + 2048)
        dma("G", sl[:, :], d_w1[Hh].rearrange("p k j -> p (k j)"),
            f"GS{mh % 2}", reads=sr, writes=sr)

    w1_load(0, 0)
    w1_load(0, 1)
    for hc in range(NHC):
        for mh in range(8):
            c0 = (mh % 2) * 2048
            sl, sr = pool1[:, c0:c0 + 2048], p1r(c0, c0 + 2048)
            hb = pbanks(mh % 2, 0, 2048)
            for k in range(NT):
                for g in range(NSEG):
                    last = (k == NT - 1 and g == NSEG - 1)
                    mk.emit("T", lambda e, mh=mh, k=k, g=g, sl=sl: e.matmul(
                        pt[mh % 2][:, g * SEG:(g + 1) * SEG],
                        sl[:, k * 128:(k + 1) * 128],
                        bufB[:, k * D + g * SEG:k * D + (g + 1) * SEG],
                        start=(k == 0), stop=(k == NT - 1)),
                        reads=sr + [B_k[k]], writes=hb, inc=last)
            h1res = p2r(mh * D, (mh + 1) * D)
            Hh = hc * 8 + mh
            mk.emit("S", lambda e, mh=mh, Hh=Hh: e.activation(
                pool2[:, mh * D:(mh + 1) * D], pt[mh % 2][:, :], AF.Gelu,
                bias=b1_c[:, Hh:Hh + 1]), reads=hb + [CR("b1")] + h1res,
                writes=h1res + hb)
            if mh + 2 < 8:
                w1_load(hc, mh + 2)
        for mhalf in range(2):
            for khl in range(8):
                kh = hc * 8 + khl
                c0 = 4096 + khl * 1024
                sl, sr = pool1[:, c0:c0 + 1024], p1r(c0, c0 + 1024)
                dma("G", sl[:, :],
                    d_w2[kh * 128:(kh + 1) * 128,
                         mhalf * 1024:(mhalf + 1) * 1024],
                    f"GW{khl}", reads=sr, writes=sr)
            if mhalf == 0 and hc + 1 < NHC:
                w1_load(hc + 1, 0)
                w1_load(hc + 1, 1)
            if mhalf == 1 and hc == NHC - 1:
                for t6 in range(8):
                    dma("G", bufB.bitcast(F32)[:, t6 * 2048:(t6 + 1) * 2048],
                        d_x[t6 * 128:(t6 + 1) * 128, :], "GX0",
                        reads=[B_k[2 * t6], B_k[2 * t6 + 1]],
                        writes=[B_k[2 * t6], B_k[2 * t6 + 1]])
                batch_fixup([B_k[i] for i in range(16)], "GX0")
            for ml in range(8):
                m = mhalf * 8 + ml
                mb = pbanks(ml % 2, 0, 2048)
                for khl in range(8):
                    c0 = 4096 + khl * 1024
                    sl, sr = pool1[:, c0:c0 + 1024], p1r(c0, c0 + 1024)
                    stop = (khl == 7 and hc != 0)
                    for g in range(NSEG):
                        last = (g == NSEG - 1)
                        mk.emit("T", lambda e, ml=ml, khl=khl, g=g, sl=sl,
                                stop=stop: e.matmul(
                            pt[ml % 2][:, g * SEG:(g + 1) * SEG],
                            sl[:, ml * 128:(ml + 1) * 128],
                            pool2[:, khl * D + g * SEG:khl * D + (g + 1) * SEG],
                            start=(khl == 0), stop=stop),
                            reads=sr + p2r(khl * D + g * SEG,
                                           khl * D + (g + 1) * SEG),
                            writes=mb, inc=last)
                if hc == 0:
                    for g in range(NSEG):
                        mk.emit("T", lambda e, m=m, g=g, ml=ml: e.matmul(
                            pt[ml % 2][:, g * SEG:(g + 1) * SEG],
                            b2_rb[0:1, m * 128:(m + 1) * 128],
                            ones_b[0:1, 0:SEG], start=False, stop=True),
                            reads=[CR("b2"), CR("ones")], writes=mb,
                            inc=(g == NSEG - 1))
                mk.emit("V", lambda e, m=m, ml=ml: e.tensor_tensor(
                    bufA[:, m * D:(m + 1) * D], pt[ml % 2][:, :],
                    bufA[:, m * D:(m + 1) * D], ALU.add),
                    reads=mb + Ak(m), writes=Ak(m) + mb)

    def x6_ap(t):
        if t < 8:
            return bufB.bitcast(F32)[:, t * 2048:(t + 1) * 2048]
        s = (t - 8) % 3
        return pool1[:, s * 4096:s * 4096 + 4096].bitcast(F32)

    def x6_res(t):
        if t < 8:
            return [B_k[2 * t], B_k[2 * t + 1]]
        s = (t - 8) % 3
        return p1r(s * 4096, s * 4096 + 4096)

    def p6_load(t):
        xt, xr = x6_ap(t), x6_res(t)
        sem = "GX0" if t < 8 else f"GX{1 + (t - 8) % 3}"
        dma("G", xt[:, :], d_x[t * 128:(t + 1) * 128, :], sem,
            reads=xr, writes=xr)

    for t in range(8, 11):
        p6_load(t)
    for t in range(NT):
        xt, xr = x6_ap(t), x6_res(t)
        oc0 = (t % 4) * 4096
        ot = pool2[:, oc0:oc0 + 4096].bitcast(F32)
        orr = p2r(oc0, oc0 + 4096)
        for g in range(4):
            slot = (t * 4 + g) % 4
            ptr = pt[0][:, slot * 256:slot * 256 + 256].bitcast(BF16)
            bres = RB[0][slot // 2]
            for kk in range(4):
                k = g * 4 + kk
                mk.emit("T", lambda e, ptr=ptr, kk=kk, k=k, t=t: e.transpose(
                    ptr[:, kk * 128:(kk + 1) * 128],
                    bufA[:, k * D + t * 128:k * D + t * 128 + 128],
                    ident_b[:, :]), reads=Ak(k, t // 8) + [CR("identb")],
                    writes=[bres], inc=(kk == 3))
            mk.emit("V", lambda e, ot=ot, g=g, ptr=ptr, xt=xt: e.tensor_tensor(
                ot[:, g * SEG:(g + 1) * SEG], ptr[:, :],
                xt[:, g * SEG:(g + 1) * SEG], ALU.add),
                reads=[bres] + xr + orr, writes=orr + [bres])
        dma("Q" if t < 12 else "G", d_out[t * 128:(t + 1) * 128, :],
            ot[:, :], f"QO{t % 4}" if t < 12 else f"QOG{t - 12}",
            reads=orr, writes=orr)
        if 11 <= t + 3 < NT:
            p6_load(t + 3)

    mk.wait_now("Q", ("QO0", "QO1", "QO2", "QO3", "QOG0", "QOG1",
                      "QOG2", "QOG3", "QTAP", "QC"))

    sems = {s: es.enter_context(nc.semaphore("sem" + s)) for s in ENG_SEMS}
    block = es.enter_context(nc.Block())
    mk.replay(block, sems)
    es.close()
    return nc


def host_layouts(inputs):
    g = lambda k: np.asarray(inputs[k], dtype=np.float32)
    A, Bm, Cm, Dm = g("L0_attn_A"), g("L0_attn_Bm"), g("L0_attn_Cm"), g("L0_attn_Dm")
    mats = np.stack([A.transpose(0, 2, 1), Bm.transpose(0, 2, 1),
                     Cm.transpose(0, 2, 1), Dm.transpose(0, 2, 1), A], axis=1)
    mats_p = np.ascontiguousarray(mats.transpose(2, 0, 1, 3))
    return {
        "in_w_p": np.ascontiguousarray(
            g("L0_attn_in_w").reshape(NT, 128, NT, 128).transpose(2, 1, 0, 3)),
        "out_w_p": np.ascontiguousarray(
            g("L0_attn_out_w").reshape(NT, 128, NT, 128).transpose(2, 1, 0, 3)),
        "w1_p": np.ascontiguousarray(
            g("L0_attn_w1").reshape(NT, 128, NHT, 128).transpose(2, 1, 0, 3)),
        "w2": np.ascontiguousarray(g("L0_attn_w2")),
        "wa_p": np.ascontiguousarray(
            g("L0_ffn_Wa").reshape(8, NT, 128, 128).transpose(0, 2, 1, 3)),
        "wb_p": np.ascontiguousarray(
            g("L0_ffn_Wb").reshape(8, 128, NT, 128).transpose(2, 1, 0, 3)),
        "mats_p": mats_p,
        "outb_p": np.ascontiguousarray(
            g("L0_attn_out_b").reshape(NT, 128).T),
        "router_p": np.ascontiguousarray(
            g("L0_ffn_router_w").reshape(NT, 128, 8).transpose(1, 0, 2)),
        "ln1w_p": np.ascontiguousarray(g("L0_attn_ln1_w").reshape(NT, 128).T),
        "ln1b_p": np.ascontiguousarray(g("L0_attn_ln1_b").reshape(NT, 128).T),
        "ln2w_p": np.ascontiguousarray(g("L0_attn_ln2_w").reshape(NT, 128).T),
        "ln2b_p": np.ascontiguousarray(g("L0_attn_ln2_b").reshape(NT, 128).T),
        "inb_p": np.ascontiguousarray(g("L0_attn_in_b").reshape(NT, 128).T),
        "b1_p": np.ascontiguousarray(g("L0_attn_b1").reshape(NHT, 128).T),
        "outb_r": np.ascontiguousarray(g("L0_attn_out_b").reshape(1, D)),
        "b2_r": np.ascontiguousarray(g("L0_attn_b2").reshape(1, D)),
    }


def kernel(**inputs):
    x = np.asarray(inputs["L0_attn_x"], dtype=np.float32)
    w = host_layouts(inputs)
    nc = build_program()
    in_maps = []
    for i in range(8):
        m = {"x": np.ascontiguousarray(x[i])}
        m.update(w)
        in_maps.append(m)
    res = run_bass_kernel_spmd(nc, in_maps, core_ids=list(range(8)))
    return np.stack([res.results[i]["out"] for i in range(8)], axis=0)


if __name__ == "__main__":
    build_program()
    print("build ok")
